# version 20
# baseline (speedup 1.0000x reference)
import numpy as np
import ml_dtypes
from contextlib import ExitStack
import concourse.bass as bass
import concourse.mybir as mybir
from concourse.bass_utils import run_bass_kernel_spmd

F32 = mybir.dt.float32
BF16 = mybir.dt.bfloat16
I32 = mybir.dt.int32
OP = mybir.AluOpType
AF = mybir.ActivationFunctionType
BF = ml_dtypes.bfloat16

B, S, H = 2, 8192, 1024
NS, CH = 4, 256
HID = 3072
SCALE = np.float32(1.0 / 127.0)
EPS = 1e-5
NPASS, TPASS, TT = 2, 1024, 512
NTILE = TPASS // TT
KH, MH, MHID = 8, 8, 24
NSLOT = 8
NSLOT3 = 2

ENG = ("sync", "gpsimd", "vector", "scalar", "tensor")


class Plan:

    def __init__(self):
        self.q = {e: [] for e in ENG}
        self.cnt = {}
        self.sems = {}
        self.phase = "setup"
        self.tags = {}
        self.name2tag = {}
        self.wait_hw = {}

    def sem(self, name):
        self.cnt[name] = 0
        return name

    def op(self, eng, fn, inc=None, n=None):
        if inc is not None:
            if n is None:
                n = 16 if eng in ("sync", "gpsimd") and "dma" in inc else 1
            self.cnt[inc] += n
        def run(e, fn=fn, inc=inc, n=n, self=self, tag=self.phase):
            ins = fn(e)
            try:
                self.name2tag[ins.ins.name] = tag
            except AttributeError:
                pass
            if inc is not None:
                ins.then_inc(self.sems[inc], n)
        self.q[eng].append(run)
        self.tags.setdefault(eng, []).append(self.phase)
        return self.cnt[inc] if inc is not None else None

    def dma(self, eng, fn, inc):
        self.cnt[inc] += 16
        def run(e, fn=fn, inc=inc, self=self, tag=self.phase):
            ins = fn(e)
            try:
                self.name2tag[ins.ins.name] = tag
            except AttributeError:
                pass
            ins.then_inc(self.sems[inc], 16)
        self.q[eng].append(run)
        self.tags.setdefault(eng, []).append(self.phase)
        return self.cnt[inc]

    def wait(self, eng, sem, val=None):
        if val is None:
            val = self.cnt[sem]
        if val <= 0:
            return
        hw = self.wait_hw.setdefault(eng, {})
        if hw.get(sem, -1) >= val:
            return
        hw[sem] = val
        def run(e, sem=sem, val=val, self=self):
            e.wait_ge(self.sems[sem], val)
        self.q[eng].append(run)
        self.tags.setdefault(eng, []).append(self.phase + ":w")


def build_nc(debug_dumps=()):
    nc = bass.Bass("TRN2", target_bir_lowering=False, debug=False,
                   detect_race_conditions=False)
    P = Plan()

    d_x = nc.dram_tensor("x_img", [NPASS, 8, 128, TPASS], F32, kind="ExternalInput").ap()
    dw = {}
    for nm in ("w_in", "w_val", "w_q", "w_rout", "w_up", "w_gate", "w_down"):
        dw[nm] = nc.dram_tensor(nm, [MH, 128, 1024], F32, kind="ExternalInput").ap()
    for nm in ("f_gate", "f_up"):
        dw[nm] = nc.dram_tensor(nm, [MHID, 128, 1024], I32, kind="ExternalInput").ap()
    dw["f_down"] = nc.dram_tensor("f_down", [MH, 128, HID], I32, kind="ExternalInput").ap()
    d_wmix = nc.dram_tensor("wmix_img", [128, 32], F32, kind="ExternalInput").ap()
    d_ldc = nc.dram_tensor("ld_cols", [128, 8], F32, kind="ExternalInput").ap()
    d_init = nc.dram_tensor("init_cols", [128, 8], F32, kind="ExternalInput").ap()
    d_mask = nc.dram_tensor("mask_img", [128, 16], F32, kind="ExternalInput").ap()
    d_y = nc.dram_tensor("y_img", [NPASS, 128, 8 * TPASS], F32, kind="ExternalOutput").ap()
    ag_in = nc.dram_tensor("ag_in", [1024], F32, kind="Internal")
    ag_out = nc.dram_tensor("ag_out", [8, 1024], F32, kind="Internal")
    d_dbg = {}
    for nm in debug_dumps:
        shp = {"ln16": [128, 8 * TPASS], "ubuf": [128, 8 * TPASS],
               "states": [128, 8 * TPASS], "read16": [128, 8 * TPASS],
               "xr1": [128, 8 * TPASS], "ug16": [128, 8 * TPASS],
               "carry": [128, 96]}[nm]
        d_dbg[nm] = nc.dram_tensor("dbg_" + nm, shp, F32, kind="ExternalOutput").ap()

    with ExitStack() as ctx:
        e = ctx.enter_context

        x_rs = e(nc.sbuf_tensor("x_rs", [128, 8 * TPASS], F32))
        ln16 = e(nc.sbuf_tensor("ln16", [128, 8 * TPASS], BF16))
        u_buf = e(nc.sbuf_tensor("u_buf", [128, 8 * TPASS], F32))
        arena = e(nc.sbuf_tensor("arena", [128, 24576], BF16))
        wslot = [e(nc.sbuf_tensor(f"wslot{i}", [128, 1024], BF16)) for i in range(NSLOT)]
        wslot3 = [e(nc.sbuf_tensor(f"wslot3_{i}", [128, HID], BF16)) for i in range(NSLOT3)]
        wmix16 = e(nc.sbuf_tensor("wmix16", [128, 32], BF16))
        d_cols = e(nc.sbuf_tensor("d_cols", [128, 8], F32))
        c_cols = e(nc.sbuf_tensor("c_cols", [128, 8], F32))
        ldc_t = e(nc.sbuf_tensor("ldc_t", [128, 8], F32))
        init_t = e(nc.sbuf_tensor("init_t", [128, 8], F32))
        mask_t = e(nc.sbuf_tensor("mask_t", [128, 16], F32))
        D1024 = e(nc.sbuf_tensor("D1024", [128, 8], F32))
        s_end = e(nc.sbuf_tensor("s_end", [128, 8], F32))
        ctmp = e(nc.sbuf_tensor("ctmp", [128, 8], F32))
        ctmp2 = e(nc.sbuf_tensor("ctmp2", [128, 8], F32))
        dm1 = e(nc.sbuf_tensor("dm1", [128, 8], F32))
        fgbuf = e(nc.sbuf_tensor("fgbuf", [128, 256], F32))
        T_all = e(nc.sbuf_tensor("T_all", [128, 64], F32))
        Tloc = e(nc.sbuf_tensor("Tloc", [128, 8], F32))
        e4 = e(nc.sbuf_tensor("e4", [4, 512], F32))
        w4 = e(nc.sbuf_tensor("w4", [1, NTILE * 4 * 512], F32))
        zrow = e(nc.sbuf_tensor("zrow", [1, 512], F32))
        z2row = e(nc.sbuf_tensor("z2row", [1, 512], F32))
        m_row = e(nc.sbuf_tensor("m_row", [1, 512], F32))
        q_row = e(nc.sbuf_tensor("q_row", [1, 512], F32))
        v_row = e(nc.sbuf_tensor("v_row", [1, 512], F32))
        r_row = e(nc.sbuf_tensor("r_row", [1, 512], F32))
        b_row = e(nc.sbuf_tensor("b_row", [1, 512], F32))
        tmp_a = [e(nc.sbuf_tensor(f"tmp_a{i}", [128, 512], F32)) for i in range(2)]
        tmp_b = [e(nc.sbuf_tensor(f"tmp_b{i}", [128, 512], F32)) for i in range(2)]
        ones1f = e(nc.sbuf_tensor("ones1f", [1, 128], F32))
        onesP16 = e(nc.sbuf_tensor("onesP16", [128, 1], BF16))
        epsc = e(nc.sbuf_tensor("epsc", [1, 1], F32))
        nhalf = e(nc.sbuf_tensor("nhalf", [1, 1], F32))
        sc1 = e(nc.sbuf_tensor("sc1", [128, 1], F32))
        sc2 = e(nc.sbuf_tensor("sc2", [128, 1], F32))

        def ar16(lo, n):
            return arena[:, lo:lo + n]
        def ar32(lo, n):
            return arena[:, lo:lo + 2 * n].bitcast(F32)

        pbank = [e(nc.psum_tensor(f"pbank{i}", [128, 512], F32)) for i in range(8)]
        ROT = (0, 1, 2, 7)
        PS_MEAN, PS_MSQ, PS_A, PS_B, PS_MIX = 3, 4, 5, 6, 7

        sem_names = ["s_pe", "s_act", "s_dve", "s_cc", "s_xtr", "s_tall", "s_e4", "s_set", "s_wmx"]
        sem_names += [f"s_ws{i}" for i in range(NSLOT)]
        sem_names += [f"s_w3s{i}" for i in range(NSLOT3)]
        sem_names += [f"s_x{p}" for p in range(8)]
        sem_names += [f"s_y{p}" for p in range(8)]
        for s in sem_names:
            P.sem(s)
            P.sems[s] = e(nc.semaphore(s))

        blk = e(nc.Block())

        bank_free = {i: ("s_dve", 0) for i in range(8)}
        slot_free = {("w", i): 0 for i in range(NSLOT)}
        slot_free.update({("w3", i): 0 for i in range(NSLOT3)})
        col_idx = [0]
        rot_idx = [0]

        def load_col(src_ap, is3072=False, nbytes_cols=1024):
            if is3072:
                i = col_idx[0] % NSLOT3
                key, sl, semn = ("w3", i), wslot3[i][:, 0:HID], f"s_w3s{i}"
            else:
                i = col_idx[0] % NSLOT
                key, sl, semn = ("w", i), wslot[i][:, 0:1024], f"s_ws{i}"
            col_idx[0] += 1
            P.wait("gpsimd", "s_pe", slot_free[key])
            tgt = P.dma("gpsimd", lambda g, sl=sl, src=src_ap: g.dma_start(out=sl, in_=src), semn)
            return sl, key, (semn, tgt)

        P.dma("sync", lambda s: s.dma_start(out=ldc_t[:], in_=d_ldc[:]), "s_set")
        P.dma("sync", lambda s: s.dma_start(out=init_t[:], in_=d_init[:]), "s_set")
        P.dma("sync", lambda s: s.dma_start(out=mask_t[:], in_=d_mask[:]), "s_set")
        P.dma("gpsimd", lambda g: g.dma_start(out=wmix16[:], in_=d_wmix[:]), "s_wmx")
        P.op("vector", lambda v: v.memset(ones1f[:], 1.0))
        P.op("vector", lambda v: v.memset(onesP16[:], 1.0))
        P.op("vector", lambda v: v.memset(epsc[:], EPS))
        P.op("vector", lambda v: v.memset(nhalf[:], -0.5))
        P.op("vector", lambda v: v.memset(sc1[:], float(SCALE)))
        P.op("vector", lambda v: v.memset(sc2[:], float(SCALE) * float(SCALE)), inc="s_dve")
        consts_done_pre = P.cnt["s_dve"]
        P.wait("scalar", "s_set")
        P.op("scalar", lambda s: s.activation(out=d_cols[:], in_=ldc_t[:], func=AF.Sigmoid), inc="s_act")
        P.wait("vector", "s_act")
        P.op("vector", lambda v: v.tensor_copy(D1024[:], d_cols[:]), inc="s_dve")
        P.wait("vector", "s_dve")
        for _ in range(10):
            P.op("vector", lambda v: v.tensor_tensor(out=D1024[:], in0=D1024[:], in1=D1024[:], op=OP.mult), inc="s_dve")
            P.wait("vector", "s_dve")
        P.op("vector", lambda v: v.tensor_copy(s_end[:], init_t[:]), inc="s_dve")
        P.wait("vector", "s_dve")
        P.op("vector", lambda v: v.tensor_scalar(out=dm1[:], in0=D1024[:],
                                                 scalar1=-1.0, scalar2=None, op0=OP.add), inc="s_dve")
        P.wait("vector", "s_dve")
        consts_done = P.cnt["s_dve"]

        x16 = u_buf[:, 0:2048].bitcast(BF16)
        sq16 = u_buf[:, 2048:4096].bitcast(BF16)
        wg32 = ar32(0, 8192)
        scr32 = ar32(16384, 1024)
        read16 = ar16(0, 8192)
        q16 = ar16(16384, 8192)
        ug16 = ar16(8192, 8192)
        h16 = ar16(0, 24576)

        def u_sl(p, t0, n=TT):
            return u_buf[:, p * TPASS + t0: p * TPASS + t0 + n]
        def x_sl(p, t0, n=TT):
            return x_rs[:, p * TPASS + t0: p * TPASS + t0 + n]
        def ln_sl(p, t0, n=TT):
            return ln16[:, p * TPASS + t0: p * TPASS + t0 + n]

        tmpa_free = {0: 0, 1: 0}
        tmpb_free = {0: 0, 1: 0}
        g_e4_prev = [0]
        store_tgt = {}

        for q in range(NPASS):
            x_tgt = {}
            for p in range(8):
                if q > 0:
                    P.wait("sync", f"s_y{p}", store_tgt[p])
                x_tgt[p] = P.dma("sync", lambda s, q=q, p=p: s.dma_start(
                    out=x_rs[:, p * TPASS:(p + 1) * TPASS], in_=d_x[q, p]), f"s_x{p}")

            P.phase = "ln1"
            ln_done_tile = []
            mix_done = []

            def ln1_cast(i):
                t0 = i * TT
                for p in range(8):
                    P.wait("scalar", f"s_x{p}", x_tgt[p])
                    P.op("scalar", lambda s, p=p, t0=t0: s.activation(
                        out=x16[:, p * 512:(p + 1) * 512],
                        in_=x_sl(p, t0), func=AF.Copy))
                    P.op("scalar", lambda s, p=p, t0=t0: s.activation(
                        out=sq16[:, p * 512:(p + 1) * 512], in_=x_sl(p, t0), func=AF.Square),
                        inc="s_act")
                return P.cnt["s_act"]

            def ln1_stats(act_cast):
                P.wait("tensor", "s_dve", consts_done)
                P.wait("tensor", "s_act", act_cast)
                P.wait("tensor", "s_dve", bank_free[PS_MEAN][1])
                P.wait("tensor", "s_dve", bank_free[PS_MSQ][1])
                for p in range(8):
                    P.op("tensor", lambda t, p=p: t.matmul(
                        pbank[PS_MEAN][0:1, :], onesP16[:, 0:1], x16[:, p * 512:(p + 1) * 512],
                        start=(p == 0), stop=(p == 7)), inc="s_pe" if p == 7 else None, n=1)
                for p in range(8):
                    P.op("tensor", lambda t, p=p: t.matmul(
                        pbank[PS_MSQ][0:1, :], onesP16[:, 0:1], sq16[:, p * 512:(p + 1) * 512],
                        start=(p == 0), stop=(p == 7)), inc="s_pe" if p == 7 else None, n=1)
                pe_stats = P.cnt["s_pe"]
                P.wait("vector", "s_pe", pe_stats)
                P.op("vector", lambda v: v.tensor_scalar(out=m_row[:], in0=pbank[PS_MEAN][0:1, :],
                                                         scalar1=1.0 / H, scalar2=None, op0=OP.mult))
                P.op("vector", lambda v: v.tensor_scalar(out=q_row[:], in0=pbank[PS_MSQ][0:1, :],
                                                         scalar1=1.0 / H, scalar2=None, op0=OP.mult))
                P.op("vector", lambda v: v.tensor_tensor(out=v_row[:], in0=m_row[:], in1=m_row[:], op=OP.mult))
                P.op("vector", lambda v: v.tensor_tensor(out=v_row[:], in0=q_row[:], in1=v_row[:], op=OP.subtract),
                     inc="s_dve")
                dve_var = P.cnt["s_dve"]
                bank_free[PS_MEAN] = ("s_dve", dve_var)
                bank_free[PS_MSQ] = ("s_dve", dve_var)
                P.wait("scalar", "s_dve", dve_var)
                P.op("scalar", lambda s: s.activation(out=r_row[:], in_=v_row[:], func=AF.Ln,
                                                      bias=epsc[0:1, 0:1]))
                P.op("scalar", lambda s: s.activation(out=r_row[:], in_=r_row[:], func=AF.Exp,
                                                      scale=nhalf[0:1, 0:1]), inc="s_act")
                act_rstd = P.cnt["s_act"]
                P.wait("vector", "s_act", act_rstd)
                P.op("vector", lambda v: v.tensor_tensor(out=b_row[:], in0=m_row[:], in1=r_row[:], op=OP.mult))
                P.op("vector", lambda v: v.tensor_scalar(out=b_row[:], in0=b_row[:],
                                                         scalar1=-1.0, scalar2=None, op0=OP.mult), inc="s_dve")
                return P.cnt["s_dve"]

            def ln1_bcast(dve_b):
                P.wait("tensor", "s_dve", dve_b)
                P.wait("tensor", "s_dve", bank_free[PS_A][1])
                P.wait("tensor", "s_dve", bank_free[PS_B][1])
                P.op("tensor", lambda t: t.matmul(pbank[PS_A][:], ones1f[:], r_row[:],
                                                  start=True, stop=True), inc="s_pe")
                P.op("tensor", lambda t: t.matmul(pbank[PS_B][:], ones1f[:], b_row[:],
                                                  start=True, stop=True), inc="s_pe")
                return P.cnt["s_pe"]

            def ln1_apply(i, pe_bc, ps):
                t0 = i * TT
                P.wait("vector", "s_pe", pe_bc)
                for p in ps:
                    P.op("vector", lambda v, p=p, t0=t0: v.tensor_tensor(
                        out=tmp_a[0][:], in0=x_sl(p, t0), in1=pbank[PS_A][:], op=OP.mult))
                    P.op("vector", lambda v, p=p, t0=t0: v.tensor_tensor(
                        out=ln_sl(p, t0), in0=tmp_a[0][:], in1=pbank[PS_B][:], op=OP.add),
                        inc="s_dve" if p == ps[-1] else None, n=1)
                return P.cnt["s_dve"]

            def ln1_finish(i, dve_ln):
                ln_done_tile.append(dve_ln)
                bank_free[PS_A] = ("s_dve", dve_ln)
                bank_free[PS_B] = ("s_dve", dve_ln)

            def mix_tile(i, dve_ln):
                t0 = i * TT
                P.wait("tensor", "s_dve", dve_ln)
                sem, val = bank_free[PS_MIX]
                P.wait("tensor", sem, val)
                P.wait("tensor", "s_wmx")
                for k in range(8):
                    P.op("tensor", lambda t, k=k, t0=t0: t.matmul(
                        pbank[PS_MIX][0:4, :], wmix16[:, k * 4:(k + 1) * 4], ln_sl(k, t0),
                        start=(k == 0), stop=(k == 7)), inc="s_pe" if k == 7 else None, n=1)
                pe_mix = P.cnt["s_pe"]
                P.wait("scalar", "s_pe", pe_mix)
                P.wait("scalar", "s_e4", g_e4_prev[0])
                P.op("scalar", lambda s: s.activation(out=e4[:], in_=pbank[PS_MIX][0:4, :],
                                                      func=AF.Exp), inc="s_act")
                act_e4 = P.cnt["s_act"]
                bank_free[PS_MIX] = ("s_act", act_e4)
                P.wait("sync", "s_act", act_e4)
                for n in range(4):
                    P.dma("sync", lambda s, n=n, i=i: s.dma_start(
                        out=w4[0:1, i * 2048 + n * 512: i * 2048 + (n + 1) * 512],
                        in_=e4[n:n + 1, :]), "s_e4")
                g_e4 = P.cnt["s_e4"]
                g_e4_prev[0] = g_e4
                P.wait("vector", "s_e4", g_e4)
                wb = i * 2048
                P.op("vector", lambda v, wb=wb: v.tensor_tensor(
                    out=zrow[:], in0=w4[0:1, wb:wb + 512], in1=w4[0:1, wb + 512:wb + 1024], op=OP.add))
                P.op("vector", lambda v, wb=wb: v.tensor_tensor(
                    out=z2row[:], in0=w4[0:1, wb + 1024:wb + 1536], in1=w4[0:1, wb + 1536:wb + 2048], op=OP.add))
                P.op("vector", lambda v: v.tensor_tensor(out=zrow[:], in0=zrow[:], in1=z2row[:], op=OP.add))
                P.op("vector", lambda v: v.reciprocal(out=z2row[:], in_=zrow[:]))
                for n in range(4):
                    P.op("vector", lambda v, wb=wb, n=n: v.tensor_tensor(
                        out=w4[0:1, wb + n * 512: wb + (n + 1) * 512],
                        in0=w4[0:1, wb + n * 512: wb + (n + 1) * 512],
                        in1=z2row[:], op=OP.mult), inc="s_dve" if n == 3 else None, n=1)
                mix_done.append(P.cnt["s_dve"])

            _ac0 = ln1_cast(0)
            _db0 = ln1_stats(_ac0)
            _bc0 = ln1_bcast(_db0)
            _dl0 = ln1_apply(0, _bc0, list(range(8)))
            ln1_finish(0, _dl0)
            mix_tile(0, _dl0)

            def pair_group(sl, rhs_fn, k_chunks=KH):
                bs = []
                for i in range(NTILE):
                    b = ROT[rot_idx[0] % len(ROT)]; rot_idx[0] += 1
                    sem, val = bank_free[b]
                    P.wait("tensor", sem, val)
                    bs.append(b)
                out = []
                for k in range(k_chunks):
                    for i in range(NTILE):
                        b = bs[i]
                        P.op("tensor", lambda t, sl=sl, k=k, rh=rhs_fn(k, i), b=b,
                             first=(k == 0), last=(k == k_chunks - 1):
                             t.matmul(pbank[b][:], sl[:, k * 128:(k + 1) * 128], rh,
                                      start=first, stop=last),
                             inc="s_pe" if k == k_chunks - 1 else None, n=1)
                        if k == k_chunks - 1:
                            out.append((b, P.cnt["s_pe"]))
                return out

            def proj(wname, out_kind, m_chunks=MH, k_chunks=KH, rhs_fn=None,
                     consume=None, is3072=False, tile_wait=None, unpair_first=0):
                for m in range(m_chunks):
                    src = dw[wname][m]
                    sl, key, wtgt = load_col(src, is3072=is3072)
                    P.wait("tensor", wtgt[0], wtgt[1])
                    if m < unpair_first:
                        for i in range(NTILE):
                            if tile_wait is not None:
                                P.wait("tensor", "s_dve", tile_wait[i])
                            b, pe_v = single_group2(sl, rhs_fn, i, k_chunks)
                            consume(m, i, b, pe_v)
                    else:
                        if tile_wait is not None:
                            for i in range(NTILE):
                                P.wait("tensor", "s_dve", tile_wait[i])
                        for i, (b, pe_v) in enumerate(pair_group(sl, rhs_fn, k_chunks)):
                            consume(m, i, b, pe_v)
                    slot_free[key] = P.cnt["s_pe"]

            def single_group2(sl, rhs_fn, i, k_chunks=KH):
                b = ROT[rot_idx[0] % len(ROT)]; rot_idx[0] += 1
                sem, val = bank_free[b]
                P.wait("tensor", sem, val)
                for k in range(k_chunks):
                    P.op("tensor", lambda t, sl=sl, k=k, rh=rhs_fn(k, i), b=b,
                         first=(k == 0), last=(k == k_chunks - 1):
                         t.matmul(pbank[b][:], sl[:, k * 128:(k + 1) * 128], rh,
                                  start=first, stop=last),
                         inc="s_pe" if k == k_chunks - 1 else None, n=1)
                return b, P.cnt["s_pe"]

            def ln_rhs(k, i):
                return ln_sl(k, i * TT)

            P.phase = "p1"
            def c_p1(m, i, b, pe_v):
                P.wait("scalar", "s_pe", pe_v)
                P.op("scalar", lambda s, m=m, i=i, b=b: s.activation(
                    out=wg32[:, (m * NTILE + i) * 512:(m * NTILE + i + 1) * 512],
                    in_=pbank[b][:], func=AF.Sigmoid), inc="s_act")
                bank_free[b] = ("s_act", P.cnt["s_act"])

            def pround(wname, i, consume):
                for m in range(MH):
                    if i == 0 and wname == "w_in":
                        if m == 1:
                            ln1_st["ac"] = ln1_cast(1)
                        elif m == 3:
                            ln1_st["db"] = ln1_stats(ln1_st["ac"])
                        elif m == 4:
                            ln1_st["bc"] = ln1_bcast(ln1_st["db"])
                        elif m == 6:
                            ln1_apply(1, ln1_st["bc"], [0, 1, 2, 3])
                        elif m == 7:
                            _dl1 = ln1_apply(1, ln1_st["bc"], [4, 5, 6, 7])
                            ln1_finish(1, _dl1)
                    sl, key, wt = load_col(dw[wname][m])
                    P.wait("tensor", wt[0], wt[1])
                    P.wait("tensor", "s_dve", ln_done_tile[i])
                    bb, pe_v = single_group2(sl, ln_rhs, i)
                    consume(m, i, bb, pe_v)
                    slot_free[key] = P.cnt["s_pe"]

            ln1_st = {}
            pround("w_in", 0, c_p1)

            P.phase = "p2"
            def c_p2(m, i, b, pe_v):
                P.wait("scalar", "s_pe", pe_v)
                P.op("scalar", lambda s, m=m, i=i, b=b: s.activation(
                    out=u_sl(m, i * TT), in_=pbank[b][:], func=AF.Silu), inc="s_act")
                av = P.cnt["s_act"]
                bank_free[b] = ("s_act", av)
                P.wait("vector", "s_act", av)
                P.op("vector", lambda v, m=m, i=i: v.tensor_tensor(
                    out=u_sl(m, i * TT), in0=u_sl(m, i * TT),
                    in1=wg32[:, (m * NTILE + i) * 512:(m * NTILE + i + 1) * 512],
                    op=OP.mult), inc="s_dve")
                if i == NTILE - 1:
                    for c in range(2):
                        init = 0.0 if c == 0 else scr32[:, c * 512 - 1:c * 512]
                        P.op("vector", lambda v, m=m, c=c, init=init: v.tensor_tensor_scan(
                            out=scr32[:, c * 512:(c + 1) * 512],
                            data0=d_cols[:, m:m + 1].broadcast_to((128, 512)),
                            data1=u_sl(m, c * 512),
                            initial=init, op0=OP.mult, op1=OP.add), inc="s_dve")
                        P.wait("vector", "s_dve")
                    P.op("vector", lambda v, m=m: v.tensor_copy(Tloc[:, m:m + 1], scr32[:, 1023:1024]),
                         inc="s_dve")
            pround("w_val", 0, c_p2)
            mix_tile(1, ln_done_tile[1])
            pround("w_in", 1, c_p1)
            pround("w_val", 1, c_p2)
            dve_T = P.cnt["s_dve"]

            def p56_m(m, pre=None):
                if pre is None:
                    sl_u, key_u, wt_u = load_col(dw["w_up"][m])
                    sl_g, key_g, wt_g = load_col(dw["w_gate"][m])
                else:
                    (sl_u, key_u, wt_u), (sl_g, key_g, wt_g) = pre
                P.wait("tensor", wt_u[0], wt_u[1])
                up_bs = pair_group(sl_u, ln_rhs)
                P.wait("tensor", wt_g[0], wt_g[1])
                g_bs = pair_group(sl_g, ln_rhs)
                for i in range(NTILE):
                    bU, pe_u = up_bs[i]
                    bG, pe_g = g_bs[i]
                    par = (m * NTILE + i) % 2
                    P.wait("scalar", "s_pe", pe_g)
                    P.wait("scalar", "s_dve", tmpb_free[par])
                    P.op("scalar", lambda s, bG=bG, par=par: s.activation(
                        out=tmp_b[par][:], in_=pbank[bG][:], func=AF.Silu), inc="s_act")
                    av = P.cnt["s_act"]
                    bank_free[bG] = ("s_act", av)
                    P.wait("vector", "s_act", av)
                    P.op("vector", lambda v, m=m, i=i, bU=bU, par=par: v.tensor_tensor(
                        out=ug16[:, (m * NTILE + i) * 512:(m * NTILE + i + 1) * 512],
                        in0=tmp_b[par][:], in1=pbank[bU][:], op=OP.mult), inc="s_dve")
                    dv = P.cnt["s_dve"]
                    bank_free[bU] = ("s_dve", dv)
                    tmpb_free[par] = dv
                slot_free[key_u] = P.cnt["s_pe"]
                slot_free[key_g] = P.cnt["s_pe"]

            P.phase = "agx"
            pre0 = (load_col(dw["w_up"][0]), load_col(dw["w_gate"][0]))
            pre1 = (load_col(dw["w_up"][1]), load_col(dw["w_gate"][1]))
            P.wait("sync", "s_dve", dve_T)
            P.dma("sync", lambda s: s.dma_start(
                out=ag_in.ap().rearrange("(r p) -> r p", p=8), in_=Tloc[:]), "s_xtr")
            x_extr = P.cnt["s_xtr"]
            P.wait("gpsimd", "s_xtr", x_extr)
            P.op("gpsimd", lambda g: g.collective_compute(
                "AllGather", OP.bypass, replica_groups=[list(range(8))],
                ins=[ag_in.ap().opt()], outs=[ag_out.ap().opt()]), inc="s_cc")
            P.phase = "p56"
            for p in range(8):
                p56_m(p, pre=(pre0 if p == 0 else pre1 if p == 1 else None))
            P.phase = "carry"
            P.wait("sync", "s_cc")
            for j in range(8):
                P.dma("sync", lambda s, j=j: s.dma_start(
                    out=T_all[:, j * 8:(j + 1) * 8],
                    in_=ag_out.ap()[j].rearrange("(r p) -> r p", p=8)), "s_tall")
            g_T = P.cnt["s_tall"]
            P.wait("vector", "s_tall", g_T)
            def cop(fn):
                P.op("vector", fn, inc="s_dve")
                P.wait("vector", "s_dve")
            for j in range(8):
                P.op("vector", lambda v, j=j: v.tensor_scalar(
                    out=fgbuf[:, j * 8:(j + 1) * 8], in0=dm1[:],
                    scalar1=mask_t[:, j:j + 1], scalar2=1.0, op0=OP.mult, op1=OP.add))
                P.op("vector", lambda v, j=j: v.tensor_scalar(
                    out=fgbuf[:, 64 + j * 8: 64 + (j + 1) * 8], in0=T_all[:, j * 8:(j + 1) * 8],
                    scalar1=mask_t[:, j:j + 1], scalar2=None, op0=OP.mult))
                if q == 0:
                    P.op("vector", lambda v, j=j: v.tensor_scalar(
                        out=fgbuf[:, 128 + j * 8: 128 + (j + 1) * 8], in0=dm1[:],
                        scalar1=mask_t[:, 8 + j:9 + j], scalar2=1.0, op0=OP.mult, op1=OP.add))
                    P.op("vector", lambda v, j=j: v.tensor_scalar(
                        out=fgbuf[:, 192 + j * 8: 192 + (j + 1) * 8], in0=T_all[:, j * 8:(j + 1) * 8],
                        scalar1=mask_t[:, 8 + j:9 + j], scalar2=None, op0=OP.mult))
            P.op("vector", lambda v: v.engine_nop() if False else v.tensor_copy(ctmp2[:], fgbuf[:, 0:8]), inc="s_dve")
            P.wait("vector", "s_dve")
            P.op("vector", lambda v: v.tensor_copy(c_cols[:], s_end[:]), inc="s_dve")
            P.wait("vector", "s_dve")
            for j in range(8):
                cop(lambda v, j=j: v.tensor_tensor(out=ctmp[:], in0=c_cols[:], in1=fgbuf[:, j * 8:(j + 1) * 8], op=OP.mult))
                cop(lambda v, j=j: v.tensor_tensor(out=c_cols[:], in0=ctmp[:], in1=fgbuf[:, 64 + j * 8: 64 + (j + 1) * 8], op=OP.add))
            if q == 0:
                for j in range(8):
                    cop(lambda v, j=j: v.tensor_tensor(out=ctmp2[:], in0=s_end[:], in1=fgbuf[:, 128 + j * 8: 128 + (j + 1) * 8], op=OP.mult))
                    cop(lambda v, j=j: v.tensor_tensor(out=s_end[:], in0=ctmp2[:], in1=fgbuf[:, 192 + j * 8: 192 + (j + 1) * 8], op=OP.add))
            P.op("vector", lambda v: v.tensor_copy(ctmp[:], c_cols[:]), inc="s_dve")
            x_ctr = 0
            carry_done = P.cnt["s_dve"]
            if q == 0 and "carry" in d_dbg:
                P.wait("sync", "s_dve", carry_done)
                P.dma("sync", lambda s: s.dma_start(out=d_dbg["carry"][0:128, 0:8], in_=Tloc[:]), "s_tall")
                P.dma("sync", lambda s: s.dma_start(out=d_dbg["carry"][0:128, 8:72], in_=T_all[:]), "s_tall")
                P.dma("sync", lambda s: s.dma_start(out=d_dbg["carry"][0:128, 72:80], in_=c_cols[:]), "s_tall")
                P.dma("sync", lambda s: s.dma_start(out=d_dbg["carry"][0:128, 80:88], in_=s_end[:]), "s_tall")
                P.dma("sync", lambda s: s.dma_start(out=d_dbg["carry"][0:128, 88:96], in_=D1024[:]), "s_tall")
            P.phase = "rescan"
            P.wait("vector", "s_dve", carry_done)
            def rescan(p):
                for c in range(2):
                    init = c_cols[:, p:p + 1] if c == 0 else u_sl(p, c * 512 - 1, 1)
                    P.op("vector", lambda v, p=p, c=c, init=init: v.tensor_tensor_scan(
                        out=u_sl(p, c * 512), data0=d_cols[:, p:p + 1].broadcast_to((128, 512)),
                        data1=u_sl(p, c * 512), initial=init, op0=OP.mult, op1=OP.add),
                        inc="s_dve")
                    P.wait("vector", "s_dve")

            P.phase = "p3"
            q_copied = {}
            for m in range(MH):
                sl, key, wt = load_col(dw["w_q"][m])
                P.wait("tensor", wt[0], wt[1])
                q_bs = pair_group(sl, ln_rhs)
                for i in range(NTILE):
                    bq, pe_q = q_bs[i]
                    P.wait("scalar", "s_pe", pe_q)
                    if m == 0 and i == 0:
                        P.wait("scalar", "s_dve", dve_T)
                    P.op("scalar", lambda s, m=m, i=i, bq=bq: s.activation(
                        out=q16[:, (m * NTILE + i) * 512:(m * NTILE + i + 1) * 512],
                        in_=pbank[bq][:], func=AF.Copy), inc="s_act")
                    q_copied[(m, i)] = P.cnt["s_act"]
                    bank_free[bq] = ("s_act", P.cnt["s_act"])
                slot_free[key] = P.cnt["s_pe"]

            for m in range(MH):
                n = m // 2
                rescan(m)
                if m % 2 == 0:
                    for i in range(NTILE):
                        psn = (PS_A, PS_B)[i]
                        sem, val = bank_free[psn]
                        P.wait("tensor", sem, val)
                        P.wait("tensor", "s_dve", mix_done[i])
                        P.op("tensor", lambda t, i=i, n=n, psn=psn: t.matmul(
                            pbank[psn][:], ones1f[:],
                            w4[0:1, i * 2048 + n * 512: i * 2048 + (n + 1) * 512],
                            start=True, stop=True), inc="s_pe")
                    smb_pe = P.cnt["s_pe"]
                for i in range(NTILE):
                    par = (m * NTILE + i) % 2
                    P.wait("vector", "s_act", q_copied[(m, i)])
                    P.wait("vector", "s_act", tmpa_free[par])
                    P.op("vector", lambda v, m=m, i=i, par=par: v.tensor_tensor(
                        out=tmp_a[par][:], in0=q16[:, (m * NTILE + i) * 512:(m * NTILE + i + 1) * 512],
                        in1=u_sl(m, i * TT), op=OP.mult), inc="s_dve")
                    dv = P.cnt["s_dve"]
                    P.wait("scalar", "s_dve", dv)
                    P.op("scalar", lambda s, par=par: s.activation(
                        out=tmp_a[par][:], in_=tmp_a[par][:], func=AF.Silu), inc="s_act")
                    av = P.cnt["s_act"]
                    tmpa_free[par] = av
                    P.wait("vector", "s_act", av)
                    P.wait("vector", "s_pe", smb_pe)
                    psn = (PS_A, PS_B)[i]
                    P.op("vector", lambda v, m=m, i=i, par=par, psn=psn: v.tensor_tensor(
                        out=read16[:, (m * NTILE + i) * 512:(m * NTILE + i + 1) * 512],
                        in0=tmp_a[par][:], in1=pbank[psn][:], op=OP.mult), inc="s_dve")
                    if m % 2 == 1:
                        bank_free[psn] = ("s_dve", P.cnt["s_dve"])

            P.wait("tensor", "s_dve")
            ln2_done = []

            def ln2_stage_cast(i, resid1_t):
                t0 = i * TT
                for p in range(8):
                    P.wait("scalar", "s_dve", resid1_t)
                    P.op("scalar", lambda s, p=p, t0=t0: s.activation(
                        out=x16[:, p * 512:(p + 1) * 512], in_=x_sl(p, t0), func=AF.Copy))
                    P.op("scalar", lambda s, p=p, t0=t0: s.activation(
                        out=sq16[:, p * 512:(p + 1) * 512], in_=x_sl(p, t0), func=AF.Square),
                        inc="s_act")
                return P.cnt["s_act"]

            def ln2_stage_stats(act_cast):
                P.wait("tensor", "s_act", act_cast)
                sem, val = bank_free[PS_MEAN]; P.wait("tensor", sem, val)
                sem, val = bank_free[PS_MSQ]; P.wait("tensor", sem, val)
                for p in range(8):
                    P.op("tensor", lambda t, p=p: t.matmul(
                        pbank[PS_MEAN][0:1, :], onesP16[:, 0:1], x16[:, p * 512:(p + 1) * 512],
                        start=(p == 0), stop=(p == 7)), inc="s_pe" if p == 7 else None, n=1)
                for p in range(8):
                    P.op("tensor", lambda t, p=p: t.matmul(
                        pbank[PS_MSQ][0:1, :], onesP16[:, 0:1], sq16[:, p * 512:(p + 1) * 512],
                        start=(p == 0), stop=(p == 7)), inc="s_pe" if p == 7 else None, n=1)
                pe_stats = P.cnt["s_pe"]
                P.wait("vector", "s_pe", pe_stats)
                P.op("vector", lambda v: v.tensor_scalar(out=m_row[:], in0=pbank[PS_MEAN][0:1, :],
                                                         scalar1=1.0 / H, scalar2=None, op0=OP.mult))
                P.op("vector", lambda v: v.tensor_scalar(out=q_row[:], in0=pbank[PS_MSQ][0:1, :],
                                                         scalar1=1.0 / H, scalar2=None, op0=OP.mult))
                P.op("vector", lambda v: v.tensor_tensor(out=v_row[:], in0=m_row[:], in1=m_row[:], op=OP.mult))
                P.op("vector", lambda v: v.tensor_tensor(out=v_row[:], in0=q_row[:], in1=v_row[:], op=OP.subtract),
                     inc="s_dve")
                dve_var = P.cnt["s_dve"]
                bank_free[PS_MEAN] = ("s_dve", dve_var)
                bank_free[PS_MSQ] = ("s_dve", dve_var)
                P.wait("scalar", "s_dve", dve_var)
                P.op("scalar", lambda s: s.activation(out=r_row[:], in_=v_row[:], func=AF.Ln,
                                                      bias=epsc[0:1, 0:1]))
                P.op("scalar", lambda s: s.activation(out=r_row[:], in_=r_row[:], func=AF.Exp,
                                                      scale=nhalf[0:1, 0:1]), inc="s_act")
                act_rstd = P.cnt["s_act"]
                P.wait("vector", "s_act", act_rstd)
                P.op("vector", lambda v: v.tensor_tensor(out=b_row[:], in0=m_row[:], in1=r_row[:], op=OP.mult))
                P.op("vector", lambda v: v.tensor_scalar(out=b_row[:], in0=b_row[:],
                                                         scalar1=-1.0, scalar2=None, op0=OP.mult), inc="s_dve")
                return P.cnt["s_dve"]

            def ln2_stage_bcast(dve_b):
                P.wait("tensor", "s_dve", dve_b)
                sem, val = bank_free[PS_A]; P.wait("tensor", sem, val)
                sem, val = bank_free[PS_B]; P.wait("tensor", sem, val)
                P.op("tensor", lambda t: t.matmul(pbank[PS_A][:], ones1f[:], r_row[:],
                                                  start=True, stop=True), inc="s_pe")
                P.op("tensor", lambda t: t.matmul(pbank[PS_B][:], ones1f[:], b_row[:],
                                                  start=True, stop=True), inc="s_pe")
                return P.cnt["s_pe"]

            def ln2_stage_apply(i, pe_bc, ps):
                t0 = i * TT
                P.wait("vector", "s_pe", pe_bc)
                for p in ps:
                    P.op("vector", lambda v, p=p, t0=t0: v.tensor_tensor(
                        out=tmp_a[0][:], in0=x_sl(p, t0), in1=pbank[PS_A][:], op=OP.mult))
                    P.op("vector", lambda v, p=p, t0=t0: v.tensor_tensor(
                        out=ln_sl(p, t0), in0=tmp_a[0][:], in1=pbank[PS_B][:], op=OP.add),
                        inc="s_dve" if p == ps[-1] else None, n=1)
                return P.cnt["s_dve"]

            ln2_st = {}
            for i in range(NTILE):
                for m in range(MH):
                    if i == 1:
                        if m == 0:
                            ln2_st["cast"] = ln2_stage_cast(0, resid1_t0)
                        elif m == 2:
                            ln2_st["db"] = ln2_stage_stats(ln2_st["cast"])
                        elif m == 3:
                            ln2_st["bc"] = ln2_stage_bcast(ln2_st["db"])
                        elif m == 5:
                            ln2_stage_apply(0, ln2_st["bc"], [0, 1, 2, 3])
                        elif m == 7:
                            dv = ln2_stage_apply(0, ln2_st["bc"], [4, 5, 6, 7])
                            ln2_done.append(dv)
                            bank_free[PS_A] = ("s_dve", dv)
                            bank_free[PS_B] = ("s_dve", dv)
                    sl_r, key_r, wt_r = load_col(dw["w_rout"][m])
                    sl_d, key_d, wt_d = load_col(dw["w_down"][m])
                    P.wait("tensor", wt_d[0], wt_d[1])
                    b = ROT[rot_idx[0] % len(ROT)]; rot_idx[0] += 1
                    sem, val = bank_free[b]; P.wait("tensor", sem, val)
                    for k in range(KH):
                        P.op("tensor", lambda t, sl=sl_r, k=k, b=b, i=i,
                             first=(k == 0):
                             t.matmul(pbank[b][:], sl[:, k * 128:(k + 1) * 128],
                                      read16[:, (k * NTILE + i) * 512:(k * NTILE + i + 1) * 512],
                                      start=first, stop=False), n=1)
                    for k in range(KH):
                        P.op("tensor", lambda t, sl=sl_d, k=k, b=b, i=i,
                             last=(k == 7):
                             t.matmul(pbank[b][:], sl[:, k * 128:(k + 1) * 128],
                                      ug16[:, (k * NTILE + i) * 512:(k * NTILE + i + 1) * 512],
                                      start=False, stop=last),
                             inc="s_pe" if k == 7 else None, n=1)
                    pe_v = P.cnt["s_pe"]
                    P.wait("vector", "s_pe", pe_v)
                    P.op("vector", lambda v, m=m, i=i, b=b: v.tensor_tensor(
                        out=x_sl(m, i * TT), in0=x_sl(m, i * TT), in1=pbank[b][:], op=OP.add),
                        inc="s_dve")
                    bank_free[b] = ("s_dve", P.cnt["s_dve"])
                    slot_free[key_r] = P.cnt["s_pe"]
                    slot_free[key_d] = P.cnt["s_pe"]
                if i == 0:
                    resid1_t0 = P.cnt["s_dve"]

            P.phase = "ffna"
            def single_group(sl, rhs_fn, i, k_chunks=KH):
                b = ROT[rot_idx[0] % len(ROT)]; rot_idx[0] += 1
                sem, val = bank_free[b]
                P.wait("tensor", sem, val)
                for k in range(k_chunks):
                    P.op("tensor", lambda t, sl=sl, k=k, rh=rhs_fn(k, i), b=b,
                         first=(k == 0), last=(k == k_chunks - 1):
                         t.matmul(pbank[b][:], sl[:, k * 128:(k + 1) * 128], rh,
                                  start=first, stop=last),
                         inc="s_pe" if k == k_chunks - 1 else None, n=1)
                return b, P.cnt["s_pe"]

            def ffna_entry(m, tiles):
                sl_g, key_g, wt_g = load_col(dw["f_gate"][m])
                sl_u, key_u, wt_u = load_col(dw["f_up"][m])
                for i in tiles:
                    P.wait("tensor", "s_dve", ln2_done[i])
                if len(tiles) == 2:
                    P.wait("tensor", wt_g[0], wt_g[1])
                    g_bs = pair_group(sl_g, ln_rhs)
                    P.wait("tensor", wt_u[0], wt_u[1])
                    u_bs = pair_group(sl_u, ln_rhs)
                else:
                    P.wait("tensor", wt_g[0], wt_g[1])
                    g_bs = [single_group(sl_g, ln_rhs, tiles[0])]
                    P.wait("tensor", wt_u[0], wt_u[1])
                    u_bs = [single_group(sl_u, ln_rhs, tiles[0])]
                for idx, i in enumerate(tiles):
                    bG, pe_g = g_bs[idx]
                    bU, pe_u = u_bs[idx]
                    par = (m * NTILE + i) % 2
                    P.wait("scalar", "s_pe", pe_g)
                    P.wait("scalar", "s_dve", tmpb_free[par])
                    P.op("scalar", lambda s, bG=bG, par=par: s.activation(
                        out=tmp_b[par][:], in_=pbank[bG][:], func=AF.Silu,
                        scale=sc1[:, 0:1]), inc="s_act")
                    av = P.cnt["s_act"]
                    bank_free[bG] = ("s_act", av)
                    P.wait("vector", "s_act", av)
                    P.wait("vector", "s_pe", pe_u)
                    P.op("vector", lambda v, m=m, i=i, bU=bU, par=par: v.tensor_tensor(
                        out=h16[:, (m * NTILE + i) * 512:(m * NTILE + i + 1) * 512],
                        in0=tmp_b[par][:], in1=pbank[bU][:], op=OP.mult), inc="s_dve")
                    dv = P.cnt["s_dve"]
                    bank_free[bU] = ("s_dve", dv)
                    tmpb_free[par] = dv
                slot_free[key_g] = P.cnt["s_pe"]
                slot_free[key_u] = P.cnt["s_pe"]

            resid1_t1 = P.cnt["s_dve"]
            ln2_st["cast1"] = ln2_stage_cast(1, resid1_t1)
            ffna_entry(0, (0,))
            ln2_st["db1"] = ln2_stage_stats(ln2_st["cast1"])
            ffna_entry(1, (0,))
            ln2_st["bc1"] = ln2_stage_bcast(ln2_st["db1"])
            ffna_entry(2, (0,))
            ln2_stage_apply(1, ln2_st["bc1"], [0, 1, 2, 3])
            ffna_entry(3, (0,))
            dv = ln2_stage_apply(1, ln2_st["bc1"], [4, 5, 6, 7])
            ln2_done.append(dv)
            bank_free[PS_A] = ("s_dve", dv)
            bank_free[PS_B] = ("s_dve", dv)
            for m in range(4, MHID):
                ffna_entry(m, (0, 1))
            for m in range(4):
                ffna_entry(m, (1,))

            P.phase = "ffnb"
            P.wait("tensor", "s_dve")
            def h_rhs(k, i):
                return h16[:, (k * NTILE + i) * 512:(k * NTILE + i + 1) * 512]
            for m in range(MH):
                sl, key, wt = load_col(dw["f_down"][m], is3072=True)
                P.wait("tensor", wt[0], wt[1])
                for i, (b, pe_v) in enumerate(pair_group(sl, h_rhs, k_chunks=MHID)):
                    P.wait("vector", "s_pe", pe_v)
                    P.op("vector", lambda v, m=m, i=i, b=b: v.scalar_tensor_tensor(
                        out=x_sl(m, i * TT), in0=pbank[b][:], scalar=sc2[:, 0:1],
                        in1=x_sl(m, i * TT), op0=OP.mult, op1=OP.add), inc="s_dve")
                    bank_free[b] = ("s_dve", P.cnt["s_dve"])
                P.wait("sync", "s_dve", P.cnt["s_dve"])
                store_tgt[m] = P.dma("sync", lambda s, m=m, q=q: s.dma_start(
                    out=d_y[q][:, m * TPASS:(m + 1) * TPASS],
                    in_=x_rs[:, m * TPASS:(m + 1) * TPASS]), f"s_y{m}")
                slot_free[key] = P.cnt["s_pe"]
            resid2_done = P.cnt["s_dve"]

            if q == 0 and d_dbg:
                P.wait("sync", "s_dve")
                for nm, dst in d_dbg.items():
                    src = {"ln16": None, "ubuf": u_buf, "states": u_buf,
                           "read16": None, "xr1": x_rs, "ug16": None}.get(nm)
                    if src is not None:
                        P.dma("sync", lambda s, src=src, dst=dst: s.dma_start(out=dst, in_=src[:]), "s_tall")


        for p in range(8):
            P.wait("sync", f"s_y{p}")

        @blk.sync
        def _(engine):
            for f in P.q["sync"]:
                f(engine)

        @blk.gpsimd
        def _(engine):
            for f in P.q["gpsimd"]:
                f(engine)

        @blk.vector
        def _(engine):
            for f in P.q["vector"]:
                f(engine)

        @blk.scalar
        def _(engine):
            for f in P.q["scalar"]:
                f(engine)

        @blk.tensor
        def _(engine):
            for f in P.q["tensor"]:
                f(engine)

    nc._plan = P
    return nc



def _img_sq(w):
    t = np.asarray(w).T.reshape(8, 128, 8, 128)
    return np.ascontiguousarray(t.transpose(2, 1, 0, 3).reshape(8, 128, 1024))


def _img_up(w):
    t = np.asarray(w).T.reshape(8, 128, 24, 128)
    return np.ascontiguousarray(t.transpose(2, 1, 0, 3).reshape(24, 128, 1024))


def _img_down(w):
    t = np.asarray(w).T.reshape(24, 128, 8, 128)
    return np.ascontiguousarray(t.transpose(2, 1, 0, 3).reshape(8, 128, 3072))


def make_inputs(core, ins):
    b, g = core // 4, core % 4
    x = np.asarray(ins["L0_attn_x"])
    xb = x[b]
    xi = np.empty((NPASS, 8, 128, TPASS), np.float32)
    for qq in range(NPASS):
        toks = xb[qq * 4096 + g * 1024: qq * 4096 + (g + 1) * 1024, :]
        xi[qq] = toks.T.reshape(8, 128, TPASS)
    ldf = np.asarray(ins["L0_attn_logit_decay"], np.float32).reshape(1024)
    inf_ = np.asarray(ins["L0_attn_initial_state"], np.float32).reshape(1024)
    wmix = np.asarray(ins["L0_attn_W_mix"], np.float32)
    wmix_img = np.ascontiguousarray(wmix.T.reshape(8, 128, 4).transpose(1, 0, 2).reshape(128, 32))
    mA = np.array([1.0 if (j // 4 == b and j % 4 < g) else 0.0 for j in range(8)], np.float32)
    mE = np.array([1.0 if j // 4 == b else 0.0 for j in range(8)], np.float32)
    mask = np.tile(np.concatenate([mA, mE])[None, :], (128, 1)).astype(np.float32)
    return {
        "x_img": xi,
        "w_in": _img_sq(ins["L0_attn_W_in"]).astype(np.float32),
        "w_val": _img_sq(ins["L0_attn_W_val"]).astype(np.float32),
        "w_q": _img_sq(ins["L0_attn_W_q"]).astype(np.float32),
        "w_rout": _img_sq(ins["L0_attn_W_rout"]).astype(np.float32),
        "w_up": _img_sq(ins["L0_attn_W_up"]).astype(np.float32),
        "w_gate": _img_sq(ins["L0_attn_W_gate"]).astype(np.float32),
        "w_down": _img_sq(ins["L0_attn_W_down"]).astype(np.float32),
        "f_gate": _img_up(ins["L0_ffn_w_gate"]).astype(np.int32),
        "f_up": _img_up(ins["L0_ffn_w_up"]).astype(np.int32),
        "f_down": _img_down(ins["L0_ffn_w_down"]).astype(np.int32),
        "wmix_img": wmix_img,
        "ld_cols": np.ascontiguousarray(ldf.reshape(8, 128).T),
        "init_cols": np.ascontiguousarray(inf_.reshape(8, 128).T),
        "mask_img": mask,
    }


def assemble(results):
    out = np.empty((B, S, H), np.float32)
    for core in range(8):
        b, g = core // 4, core % 4
        y = results[core]["y_img"]
        for qq in range(NPASS):
            blkv = y[qq].reshape(128, 8, TPASS).transpose(1, 0, 2).reshape(1024, TPASS)
            out[b, qq * 4096 + g * 1024: qq * 4096 + (g + 1) * 1024, :] = blkv.T
    return out


_NC_CACHE = {}


def kernel(**inputs):
    key = "main"
    if key not in _NC_CACHE:
        _NC_CACHE[key] = build_nc()
    nc = _NC_CACHE[key]
    in_maps = [make_inputs(c, inputs) for c in range(8)]
    res = run_bass_kernel_spmd(nc, in_maps, core_ids=list(range(8)))
    return assemble(res.results)


# version 21
# speedup vs baseline: 1.0198x; 1.0198x over previous
import numpy as np
import ml_dtypes
from contextlib import ExitStack
import concourse.bass as bass
import concourse.mybir as mybir
from concourse.bass_utils import run_bass_kernel_spmd

F32 = mybir.dt.float32
BF16 = mybir.dt.bfloat16
I32 = mybir.dt.int32
OP = mybir.AluOpType
AF = mybir.ActivationFunctionType
BF = ml_dtypes.bfloat16

B, S, H = 2, 8192, 1024
NS, CH = 4, 256
HID = 3072
SCALE = np.float32(1.0 / 127.0)
EPS = 1e-5
NPASS, TPASS, TT = 2, 1024, 512
NTILE = TPASS // TT
KH, MH, MHID = 8, 8, 24
NSLOT = 8
NSLOT3 = 2

ENG = ("sync", "gpsimd", "vector", "scalar", "tensor")


class Plan:

    def __init__(self):
        self.q = {e: [] for e in ENG}
        self.cnt = {}
        self.sems = {}
        self.phase = "setup"
        self.tags = {}
        self.name2tag = {}
        self.wait_hw = {}

    def sem(self, name):
        self.cnt[name] = 0
        return name

    def op(self, eng, fn, inc=None, n=None):
        if inc is not None:
            if n is None:
                n = 16 if eng in ("sync", "gpsimd") and "dma" in inc else 1
            self.cnt[inc] += n
        def run(e, fn=fn, inc=inc, n=n, self=self, tag=self.phase):
            ins = fn(e)
            try:
                self.name2tag[ins.ins.name] = tag
            except AttributeError:
                pass
            if inc is not None:
                ins.then_inc(self.sems[inc], n)
        self.q[eng].append(run)
        self.tags.setdefault(eng, []).append(self.phase)
        return self.cnt[inc] if inc is not None else None

    def dma(self, eng, fn, inc):
        self.cnt[inc] += 16
        def run(e, fn=fn, inc=inc, self=self, tag=self.phase):
            ins = fn(e)
            try:
                self.name2tag[ins.ins.name] = tag
            except AttributeError:
                pass
            ins.then_inc(self.sems[inc], 16)
        self.q[eng].append(run)
        self.tags.setdefault(eng, []).append(self.phase)
        return self.cnt[inc]

    def wait(self, eng, sem, val=None):
        if val is None:
            val = self.cnt[sem]
        if val <= 0:
            return
        hw = self.wait_hw.setdefault(eng, {})
        if hw.get(sem, -1) >= val:
            return
        hw[sem] = val
        def run(e, sem=sem, val=val, self=self):
            e.wait_ge(self.sems[sem], val)
        self.q[eng].append(run)
        self.tags.setdefault(eng, []).append(self.phase + ":w")


def build_nc(debug_dumps=()):
    nc = bass.Bass("TRN2", target_bir_lowering=False, debug=False,
                   detect_race_conditions=False)
    P = Plan()

    d_x = nc.dram_tensor("x_img", [NPASS, 8, 128, TPASS], F32, kind="ExternalInput").ap()
    dw = {}
    for nm in ("w_in", "w_val", "w_q", "w_rout", "w_up", "w_gate", "w_down"):
        dw[nm] = nc.dram_tensor(nm, [MH, 128, 1024], F32, kind="ExternalInput").ap()
    for nm in ("f_gate", "f_up"):
        dw[nm] = nc.dram_tensor(nm, [MHID, 128, 1024], I32, kind="ExternalInput").ap()
    dw["f_down"] = nc.dram_tensor("f_down", [MH, 128, HID], I32, kind="ExternalInput").ap()
    d_wmix = nc.dram_tensor("wmix_img", [128, 32], F32, kind="ExternalInput").ap()
    d_ldc = nc.dram_tensor("ld_cols", [128, 8], F32, kind="ExternalInput").ap()
    d_init = nc.dram_tensor("init_cols", [128, 8], F32, kind="ExternalInput").ap()
    d_mask = nc.dram_tensor("mask_img", [128, 16], F32, kind="ExternalInput").ap()
    d_y = nc.dram_tensor("y_img", [NPASS, 128, 8 * TPASS], F32, kind="ExternalOutput").ap()
    ag_in = nc.dram_tensor("ag_in", [1024], F32, kind="Internal")
    ag_out = nc.dram_tensor("ag_out", [8, 1024], F32, kind="Internal")
    d_dbg = {}
    for nm in debug_dumps:
        shp = {"ln16": [128, 8 * TPASS], "ubuf": [128, 8 * TPASS],
               "states": [128, 8 * TPASS], "read16": [128, 8 * TPASS],
               "xr1": [128, 8 * TPASS], "ug16": [128, 8 * TPASS],
               "carry": [128, 96]}[nm]
        d_dbg[nm] = nc.dram_tensor("dbg_" + nm, shp, F32, kind="ExternalOutput").ap()

    with ExitStack() as ctx:
        e = ctx.enter_context

        x_rs = e(nc.sbuf_tensor("x_rs", [128, 8 * TPASS], F32))
        ln16 = e(nc.sbuf_tensor("ln16", [128, 8 * TPASS], BF16))
        u_buf = e(nc.sbuf_tensor("u_buf", [128, 8 * TPASS], F32))
        arena = e(nc.sbuf_tensor("arena", [128, 24576], BF16))
        wslot = [e(nc.sbuf_tensor(f"wslot{i}", [128, 1024], BF16)) for i in range(NSLOT)]
        wslot3 = [e(nc.sbuf_tensor(f"wslot3_{i}", [128, HID], BF16)) for i in range(NSLOT3)]
        wmix16 = e(nc.sbuf_tensor("wmix16", [128, 32], BF16))
        d_cols = e(nc.sbuf_tensor("d_cols", [128, 8], F32))
        c_cols = e(nc.sbuf_tensor("c_cols", [128, 8], F32))
        ldc_t = e(nc.sbuf_tensor("ldc_t", [128, 8], F32))
        init_t = e(nc.sbuf_tensor("init_t", [128, 8], F32))
        mask_t = e(nc.sbuf_tensor("mask_t", [128, 16], F32))
        D1024 = e(nc.sbuf_tensor("D1024", [128, 8], F32))
        s_end = e(nc.sbuf_tensor("s_end", [128, 8], F32))
        ctmp = e(nc.sbuf_tensor("ctmp", [128, 8], F32))
        ctmp2 = e(nc.sbuf_tensor("ctmp2", [128, 8], F32))
        dm1 = e(nc.sbuf_tensor("dm1", [128, 8], F32))
        fgbuf = e(nc.sbuf_tensor("fgbuf", [128, 256], F32))
        T_all = e(nc.sbuf_tensor("T_all", [128, 64], F32))
        Tloc = e(nc.sbuf_tensor("Tloc", [128, 8], F32))
        e4 = e(nc.sbuf_tensor("e4", [4, 512], F32))
        w4 = e(nc.sbuf_tensor("w4", [1, NTILE * 4 * 512], F32))
        zrow = e(nc.sbuf_tensor("zrow", [1, 512], F32))
        z2row = e(nc.sbuf_tensor("z2row", [1, 512], F32))
        m_row = e(nc.sbuf_tensor("m_row", [1, 512], F32))
        q_row = e(nc.sbuf_tensor("q_row", [1, 512], F32))
        v_row = e(nc.sbuf_tensor("v_row", [1, 512], F32))
        r_row = e(nc.sbuf_tensor("r_row", [1, 512], F32))
        b_row = e(nc.sbuf_tensor("b_row", [1, 512], F32))
        tmp_a = [e(nc.sbuf_tensor(f"tmp_a{i}", [128, 512], F32)) for i in range(2)]
        tmp_b = [e(nc.sbuf_tensor(f"tmp_b{i}", [128, 512], F32)) for i in range(2)]
        ones1f = e(nc.sbuf_tensor("ones1f", [1, 128], F32))
        onesP16 = e(nc.sbuf_tensor("onesP16", [128, 1], BF16))
        epsc = e(nc.sbuf_tensor("epsc", [1, 1], F32))
        nhalf = e(nc.sbuf_tensor("nhalf", [1, 1], F32))
        sc1 = e(nc.sbuf_tensor("sc1", [128, 1], F32))
        sc2 = e(nc.sbuf_tensor("sc2", [128, 1], F32))

        def ar16(lo, n):
            return arena[:, lo:lo + n]
        def ar32(lo, n):
            return arena[:, lo:lo + 2 * n].bitcast(F32)

        pbank = [e(nc.psum_tensor(f"pbank{i}", [128, 512], F32)) for i in range(8)]
        ROT = (0, 1, 2, 7)
        PS_MEAN, PS_MSQ, PS_A, PS_B, PS_MIX = 3, 4, 5, 6, 7

        sem_names = ["s_pe", "s_act", "s_dve", "s_cc", "s_xtr", "s_tall", "s_e4", "s_set", "s_wmx"]
        sem_names += [f"s_ws{i}" for i in range(NSLOT)]
        sem_names += [f"s_w3s{i}" for i in range(NSLOT3)]
        sem_names += [f"s_x{p}" for p in range(8)]
        sem_names += [f"s_y{p}" for p in range(8)]
        for s in sem_names:
            P.sem(s)
            P.sems[s] = e(nc.semaphore(s))

        blk = e(nc.Block())

        bank_free = {i: ("s_dve", 0) for i in range(8)}
        slot_free = {("w", i): 0 for i in range(NSLOT)}
        slot_free.update({("w3", i): 0 for i in range(NSLOT3)})
        col_idx = [0]
        rot_idx = [0]

        def load_col(src_ap, is3072=False, nbytes_cols=1024):
            if is3072:
                i = col_idx[0] % NSLOT3
                key, sl, semn = ("w3", i), wslot3[i][:, 0:HID], f"s_w3s{i}"
            else:
                i = col_idx[0] % NSLOT
                key, sl, semn = ("w", i), wslot[i][:, 0:1024], f"s_ws{i}"
            col_idx[0] += 1
            P.wait("gpsimd", "s_pe", slot_free[key])
            tgt = P.dma("gpsimd", lambda g, sl=sl, src=src_ap: g.dma_start(out=sl, in_=src), semn)
            return sl, key, (semn, tgt)

        P.dma("sync", lambda s: s.dma_start(out=ldc_t[:], in_=d_ldc[:]), "s_set")
        P.dma("sync", lambda s: s.dma_start(out=init_t[:], in_=d_init[:]), "s_set")
        P.dma("sync", lambda s: s.dma_start(out=mask_t[:], in_=d_mask[:]), "s_set")
        P.dma("gpsimd", lambda g: g.dma_start(out=wmix16[:], in_=d_wmix[:]), "s_wmx")
        P.op("vector", lambda v: v.memset(ones1f[:], 1.0))
        P.op("vector", lambda v: v.memset(onesP16[:], 1.0))
        P.op("vector", lambda v: v.memset(epsc[:], EPS))
        P.op("vector", lambda v: v.memset(nhalf[:], -0.5))
        P.op("vector", lambda v: v.memset(sc1[:], float(SCALE)))
        P.op("vector", lambda v: v.memset(sc2[:], float(SCALE) * float(SCALE)), inc="s_dve")
        consts_done_pre = P.cnt["s_dve"]
        P.wait("scalar", "s_set")
        P.op("scalar", lambda s: s.activation(out=d_cols[:], in_=ldc_t[:], func=AF.Sigmoid), inc="s_act")
        P.wait("vector", "s_act")
        P.op("vector", lambda v: v.tensor_copy(D1024[:], d_cols[:]), inc="s_dve")
        P.wait("vector", "s_dve")
        for _ in range(10):
            P.op("vector", lambda v: v.tensor_tensor(out=D1024[:], in0=D1024[:], in1=D1024[:], op=OP.mult), inc="s_dve")
            P.wait("vector", "s_dve")
        P.op("vector", lambda v: v.tensor_copy(s_end[:], init_t[:]), inc="s_dve")
        P.wait("vector", "s_dve")
        P.op("vector", lambda v: v.tensor_scalar(out=dm1[:], in0=D1024[:],
                                                 scalar1=-1.0, scalar2=None, op0=OP.add), inc="s_dve")
        P.wait("vector", "s_dve")
        consts_done = P.cnt["s_dve"]

        x16 = u_buf[:, 0:2048].bitcast(BF16)
        sq16 = u_buf[:, 2048:4096].bitcast(BF16)
        wg32 = ar32(0, 8192)
        scr32 = ar32(16384, 1024)
        read16 = ar16(0, 8192)
        q16 = ar16(16384, 8192)
        ug16 = ar16(8192, 8192)
        h16 = ar16(0, 24576)

        def u_sl(p, t0, n=TT):
            return u_buf[:, p * TPASS + t0: p * TPASS + t0 + n]
        def x_sl(p, t0, n=TT):
            return x_rs[:, p * TPASS + t0: p * TPASS + t0 + n]
        def ln_sl(p, t0, n=TT):
            return ln16[:, p * TPASS + t0: p * TPASS + t0 + n]

        tmpa_free = {0: 0, 1: 0}
        tmpb_free = {0: 0, 1: 0}
        g_e4_prev = [0]
        store_tgt = {}

        for q in range(NPASS):
            x_tgt = {}
            for p in range(8):
                if q > 0:
                    P.wait("sync", f"s_y{p}", store_tgt[p])
                x_tgt[p] = P.dma("sync", lambda s, q=q, p=p: s.dma_start(
                    out=x_rs[:, p * TPASS:(p + 1) * TPASS], in_=d_x[q, p]), f"s_x{p}")

            P.phase = "ln1"
            ln_done_tile = []
            mix_done = []

            def ln1_cast(i):
                t0 = i * TT
                for p in range(8):
                    P.wait("scalar", f"s_x{p}", x_tgt[p])
                    P.op("scalar", lambda s, p=p, t0=t0: s.activation(
                        out=x16[:, p * 512:(p + 1) * 512],
                        in_=x_sl(p, t0), func=AF.Copy))
                    P.op("scalar", lambda s, p=p, t0=t0: s.activation(
                        out=sq16[:, p * 512:(p + 1) * 512], in_=x_sl(p, t0), func=AF.Square),
                        inc="s_act")
                return P.cnt["s_act"]

            def ln1_stats(act_cast):
                P.wait("tensor", "s_dve", consts_done)
                P.wait("tensor", "s_act", act_cast)
                P.wait("tensor", "s_dve", bank_free[PS_MEAN][1])
                P.wait("tensor", "s_dve", bank_free[PS_MSQ][1])
                for p in range(8):
                    P.op("tensor", lambda t, p=p: t.matmul(
                        pbank[PS_MEAN][0:1, :], onesP16[:, 0:1], x16[:, p * 512:(p + 1) * 512],
                        start=(p == 0), stop=(p == 7)), inc="s_pe" if p == 7 else None, n=1)
                for p in range(8):
                    P.op("tensor", lambda t, p=p: t.matmul(
                        pbank[PS_MSQ][0:1, :], onesP16[:, 0:1], sq16[:, p * 512:(p + 1) * 512],
                        start=(p == 0), stop=(p == 7)), inc="s_pe" if p == 7 else None, n=1)
                pe_stats = P.cnt["s_pe"]
                P.wait("vector", "s_pe", pe_stats)
                P.op("vector", lambda v: v.tensor_scalar(out=m_row[:], in0=pbank[PS_MEAN][0:1, :],
                                                         scalar1=1.0 / H, scalar2=None, op0=OP.mult))
                P.op("vector", lambda v: v.tensor_scalar(out=q_row[:], in0=pbank[PS_MSQ][0:1, :],
                                                         scalar1=1.0 / H, scalar2=None, op0=OP.mult))
                P.op("vector", lambda v: v.tensor_tensor(out=v_row[:], in0=m_row[:], in1=m_row[:], op=OP.mult))
                P.op("vector", lambda v: v.tensor_tensor(out=v_row[:], in0=q_row[:], in1=v_row[:], op=OP.subtract),
                     inc="s_dve")
                dve_var = P.cnt["s_dve"]
                bank_free[PS_MEAN] = ("s_dve", dve_var)
                bank_free[PS_MSQ] = ("s_dve", dve_var)
                P.wait("scalar", "s_dve", dve_var)
                P.op("scalar", lambda s: s.activation(out=r_row[:], in_=v_row[:], func=AF.Ln,
                                                      bias=epsc[0:1, 0:1]))
                P.op("scalar", lambda s: s.activation(out=r_row[:], in_=r_row[:], func=AF.Exp,
                                                      scale=nhalf[0:1, 0:1]), inc="s_act")
                act_rstd = P.cnt["s_act"]
                P.wait("vector", "s_act", act_rstd)
                P.op("vector", lambda v: v.tensor_tensor(out=b_row[:], in0=m_row[:], in1=r_row[:], op=OP.mult))
                P.op("vector", lambda v: v.tensor_scalar(out=b_row[:], in0=b_row[:],
                                                         scalar1=-1.0, scalar2=None, op0=OP.mult), inc="s_dve")
                return P.cnt["s_dve"]

            def ln1_bcast(dve_b):
                P.wait("tensor", "s_dve", dve_b)
                P.wait("tensor", "s_dve", bank_free[PS_A][1])
                P.wait("tensor", "s_dve", bank_free[PS_B][1])
                P.op("tensor", lambda t: t.matmul(pbank[PS_A][:], ones1f[:], r_row[:],
                                                  start=True, stop=True), inc="s_pe")
                P.op("tensor", lambda t: t.matmul(pbank[PS_B][:], ones1f[:], b_row[:],
                                                  start=True, stop=True), inc="s_pe")
                return P.cnt["s_pe"]

            def ln1_apply(i, pe_bc, ps):
                t0 = i * TT
                P.wait("vector", "s_pe", pe_bc)
                for p in ps:
                    P.op("vector", lambda v, p=p, t0=t0: v.tensor_tensor(
                        out=tmp_a[0][:], in0=x_sl(p, t0), in1=pbank[PS_A][:], op=OP.mult))
                    P.op("vector", lambda v, p=p, t0=t0: v.tensor_tensor(
                        out=ln_sl(p, t0), in0=tmp_a[0][:], in1=pbank[PS_B][:], op=OP.add),
                        inc="s_dve" if p == ps[-1] else None, n=1)
                return P.cnt["s_dve"]

            def ln1_finish(i, dve_ln):
                ln_done_tile.append(dve_ln)
                bank_free[PS_A] = ("s_dve", dve_ln)
                bank_free[PS_B] = ("s_dve", dve_ln)

            def mix_tile(i, dve_ln):
                t0 = i * TT
                P.wait("tensor", "s_dve", dve_ln)
                sem, val = bank_free[PS_MIX]
                P.wait("tensor", sem, val)
                P.wait("tensor", "s_wmx")
                for k in range(8):
                    P.op("tensor", lambda t, k=k, t0=t0: t.matmul(
                        pbank[PS_MIX][0:4, :], wmix16[:, k * 4:(k + 1) * 4], ln_sl(k, t0),
                        start=(k == 0), stop=(k == 7)), inc="s_pe" if k == 7 else None, n=1)
                pe_mix = P.cnt["s_pe"]
                P.wait("scalar", "s_pe", pe_mix)
                P.wait("scalar", "s_e4", g_e4_prev[0])
                P.op("scalar", lambda s: s.activation(out=e4[:], in_=pbank[PS_MIX][0:4, :],
                                                      func=AF.Exp), inc="s_act")
                act_e4 = P.cnt["s_act"]
                bank_free[PS_MIX] = ("s_act", act_e4)
                P.wait("sync", "s_act", act_e4)
                for n in range(4):
                    P.dma("sync", lambda s, n=n, i=i: s.dma_start(
                        out=w4[0:1, i * 2048 + n * 512: i * 2048 + (n + 1) * 512],
                        in_=e4[n:n + 1, :]), "s_e4")
                g_e4 = P.cnt["s_e4"]
                g_e4_prev[0] = g_e4
                P.wait("vector", "s_e4", g_e4)
                wb = i * 2048
                P.op("vector", lambda v, wb=wb: v.tensor_tensor(
                    out=zrow[:], in0=w4[0:1, wb:wb + 512], in1=w4[0:1, wb + 512:wb + 1024], op=OP.add))
                P.op("vector", lambda v, wb=wb: v.tensor_tensor(
                    out=z2row[:], in0=w4[0:1, wb + 1024:wb + 1536], in1=w4[0:1, wb + 1536:wb + 2048], op=OP.add))
                P.op("vector", lambda v: v.tensor_tensor(out=zrow[:], in0=zrow[:], in1=z2row[:], op=OP.add))
                P.op("vector", lambda v: v.reciprocal(out=z2row[:], in_=zrow[:]))
                for n in range(4):
                    P.op("vector", lambda v, wb=wb, n=n: v.tensor_tensor(
                        out=w4[0:1, wb + n * 512: wb + (n + 1) * 512],
                        in0=w4[0:1, wb + n * 512: wb + (n + 1) * 512],
                        in1=z2row[:], op=OP.mult), inc="s_dve" if n == 3 else None, n=1)
                mix_done.append(P.cnt["s_dve"])

            _ac0 = ln1_cast(0)
            _db0 = ln1_stats(_ac0)
            _bc0 = ln1_bcast(_db0)
            _dl0 = ln1_apply(0, _bc0, list(range(8)))
            ln1_finish(0, _dl0)
            mix_tile(0, _dl0)

            def pair_group(sl, rhs_fn, k_chunks=KH):
                bs = []
                for i in range(NTILE):
                    b = ROT[rot_idx[0] % len(ROT)]; rot_idx[0] += 1
                    sem, val = bank_free[b]
                    P.wait("tensor", sem, val)
                    bs.append(b)
                out = []
                for k in range(k_chunks):
                    for i in range(NTILE):
                        b = bs[i]
                        P.op("tensor", lambda t, sl=sl, k=k, rh=rhs_fn(k, i), b=b,
                             first=(k == 0), last=(k == k_chunks - 1):
                             t.matmul(pbank[b][:], sl[:, k * 128:(k + 1) * 128], rh,
                                      start=first, stop=last),
                             inc="s_pe" if k == k_chunks - 1 else None, n=1)
                        if k == k_chunks - 1:
                            out.append((b, P.cnt["s_pe"]))
                return out

            def proj(wname, out_kind, m_chunks=MH, k_chunks=KH, rhs_fn=None,
                     consume=None, is3072=False, tile_wait=None, unpair_first=0):
                for m in range(m_chunks):
                    src = dw[wname][m]
                    sl, key, wtgt = load_col(src, is3072=is3072)
                    P.wait("tensor", wtgt[0], wtgt[1])
                    if m < unpair_first:
                        for i in range(NTILE):
                            if tile_wait is not None:
                                P.wait("tensor", "s_dve", tile_wait[i])
                            b, pe_v = single_group2(sl, rhs_fn, i, k_chunks)
                            consume(m, i, b, pe_v)
                    else:
                        if tile_wait is not None:
                            for i in range(NTILE):
                                P.wait("tensor", "s_dve", tile_wait[i])
                        for i, (b, pe_v) in enumerate(pair_group(sl, rhs_fn, k_chunks)):
                            consume(m, i, b, pe_v)
                    slot_free[key] = P.cnt["s_pe"]

            def single_group2(sl, rhs_fn, i, k_chunks=KH):
                b = ROT[rot_idx[0] % len(ROT)]; rot_idx[0] += 1
                sem, val = bank_free[b]
                P.wait("tensor", sem, val)
                for k in range(k_chunks):
                    P.op("tensor", lambda t, sl=sl, k=k, rh=rhs_fn(k, i), b=b,
                         first=(k == 0), last=(k == k_chunks - 1):
                         t.matmul(pbank[b][:], sl[:, k * 128:(k + 1) * 128], rh,
                                  start=first, stop=last),
                         inc="s_pe" if k == k_chunks - 1 else None, n=1)
                return b, P.cnt["s_pe"]

            def ln_rhs(k, i):
                return ln_sl(k, i * TT)

            P.phase = "p1"
            def c_p1(m, i, b, pe_v):
                P.wait("scalar", "s_pe", pe_v)
                P.op("scalar", lambda s, m=m, i=i, b=b: s.activation(
                    out=wg32[:, (m * NTILE + i) * 512:(m * NTILE + i + 1) * 512],
                    in_=pbank[b][:], func=AF.Sigmoid), inc="s_act")
                bank_free[b] = ("s_act", P.cnt["s_act"])

            def pround(wname, i, consume):
                for m in range(MH):
                    if i == 0 and wname == "w_in":
                        if m == 1:
                            ln1_st["ac"] = ln1_cast(1)
                        elif m == 3:
                            ln1_st["db"] = ln1_stats(ln1_st["ac"])
                        elif m == 4:
                            ln1_st["bc"] = ln1_bcast(ln1_st["db"])
                        elif m == 6:
                            ln1_apply(1, ln1_st["bc"], [0, 1, 2, 3])
                        elif m == 7:
                            _dl1 = ln1_apply(1, ln1_st["bc"], [4, 5, 6, 7])
                            ln1_finish(1, _dl1)
                    sl, key, wt = load_col(dw[wname][m])
                    P.wait("tensor", wt[0], wt[1])
                    P.wait("tensor", "s_dve", ln_done_tile[i])
                    bb, pe_v = single_group2(sl, ln_rhs, i)
                    consume(m, i, bb, pe_v)
                    slot_free[key] = P.cnt["s_pe"]

            ln1_st = {}
            pround("w_in", 0, c_p1)

            P.phase = "p2"
            def c_p2(m, i, b, pe_v):
                P.wait("scalar", "s_pe", pe_v)
                P.op("scalar", lambda s, m=m, i=i, b=b: s.activation(
                    out=u_sl(m, i * TT), in_=pbank[b][:], func=AF.Silu), inc="s_act")
                av = P.cnt["s_act"]
                bank_free[b] = ("s_act", av)
                P.wait("vector", "s_act", av)
                P.op("vector", lambda v, m=m, i=i: v.tensor_tensor(
                    out=u_sl(m, i * TT), in0=u_sl(m, i * TT),
                    in1=wg32[:, (m * NTILE + i) * 512:(m * NTILE + i + 1) * 512],
                    op=OP.mult), inc="s_dve")
                if i == NTILE - 1:
                    for c in range(2):
                        init = 0.0 if c == 0 else scr32[:, c * 512 - 1:c * 512]
                        P.op("vector", lambda v, m=m, c=c, init=init: v.tensor_tensor_scan(
                            out=scr32[:, c * 512:(c + 1) * 512],
                            data0=d_cols[:, m:m + 1].broadcast_to((128, 512)),
                            data1=u_sl(m, c * 512),
                            initial=init, op0=OP.mult, op1=OP.add), inc="s_dve")
                        P.wait("vector", "s_dve")
                    P.op("vector", lambda v, m=m: v.tensor_copy(Tloc[:, m:m + 1], scr32[:, 1023:1024]),
                         inc="s_dve")
            pround("w_val", 0, c_p2)
            mix_tile(1, ln_done_tile[1])
            pround("w_in", 1, c_p1)
            pround("w_val", 1, c_p2)
            dve_T = P.cnt["s_dve"]

            def p56_m(m, pre=None):
                if pre is None:
                    sl_u, key_u, wt_u = load_col(dw["w_up"][m])
                    sl_g, key_g, wt_g = load_col(dw["w_gate"][m])
                else:
                    (sl_u, key_u, wt_u), (sl_g, key_g, wt_g) = pre
                P.wait("tensor", wt_u[0], wt_u[1])
                up_bs = pair_group(sl_u, ln_rhs)
                P.wait("tensor", wt_g[0], wt_g[1])
                g_bs = pair_group(sl_g, ln_rhs)
                for i in range(NTILE):
                    bU, pe_u = up_bs[i]
                    bG, pe_g = g_bs[i]
                    par = (m * NTILE + i) % 2
                    P.wait("scalar", "s_pe", pe_g)
                    P.wait("scalar", "s_dve", tmpb_free[par])
                    P.op("scalar", lambda s, bG=bG, par=par: s.activation(
                        out=tmp_b[par][:], in_=pbank[bG][:], func=AF.Silu), inc="s_act")
                    av = P.cnt["s_act"]
                    bank_free[bG] = ("s_act", av)
                    P.wait("vector", "s_act", av)
                    P.op("vector", lambda v, m=m, i=i, bU=bU, par=par: v.tensor_tensor(
                        out=ug16[:, (m * NTILE + i) * 512:(m * NTILE + i + 1) * 512],
                        in0=tmp_b[par][:], in1=pbank[bU][:], op=OP.mult), inc="s_dve")
                    dv = P.cnt["s_dve"]
                    bank_free[bU] = ("s_dve", dv)
                    tmpb_free[par] = dv
                slot_free[key_u] = P.cnt["s_pe"]
                slot_free[key_g] = P.cnt["s_pe"]

            P.phase = "agx"
            pre0 = (load_col(dw["w_up"][0]), load_col(dw["w_gate"][0]))
            pre1 = (load_col(dw["w_up"][1]), load_col(dw["w_gate"][1]))
            P.wait("sync", "s_dve", dve_T)
            P.dma("sync", lambda s: s.dma_start(
                out=ag_in.ap().rearrange("(r p) -> r p", p=8), in_=Tloc[:]), "s_xtr")
            x_extr = P.cnt["s_xtr"]
            P.wait("gpsimd", "s_xtr", x_extr)
            P.op("gpsimd", lambda g: g.collective_compute(
                "AllGather", OP.bypass, replica_groups=[list(range(8))],
                ins=[ag_in.ap().opt()], outs=[ag_out.ap().opt()]), inc="s_cc")
            P.phase = "p56"
            for p in range(8):
                p56_m(p, pre=(pre0 if p == 0 else pre1 if p == 1 else None))
            ug_done = P.cnt["s_dve"]
            P.phase = "carry"
            P.wait("sync", "s_cc")
            for j in range(8):
                P.dma("sync", lambda s, j=j: s.dma_start(
                    out=T_all[:, j * 8:(j + 1) * 8],
                    in_=ag_out.ap()[j].rearrange("(r p) -> r p", p=8)), "s_tall")
            g_T = P.cnt["s_tall"]
            P.wait("vector", "s_tall", g_T)
            def cop(fn):
                P.op("vector", fn, inc="s_dve")
                P.wait("vector", "s_dve")
            for j in range(8):
                P.op("vector", lambda v, j=j: v.tensor_scalar(
                    out=fgbuf[:, j * 8:(j + 1) * 8], in0=dm1[:],
                    scalar1=mask_t[:, j:j + 1], scalar2=1.0, op0=OP.mult, op1=OP.add))
                P.op("vector", lambda v, j=j: v.tensor_scalar(
                    out=fgbuf[:, 64 + j * 8: 64 + (j + 1) * 8], in0=T_all[:, j * 8:(j + 1) * 8],
                    scalar1=mask_t[:, j:j + 1], scalar2=None, op0=OP.mult))
                if q == 0:
                    P.op("vector", lambda v, j=j: v.tensor_scalar(
                        out=fgbuf[:, 128 + j * 8: 128 + (j + 1) * 8], in0=dm1[:],
                        scalar1=mask_t[:, 8 + j:9 + j], scalar2=1.0, op0=OP.mult, op1=OP.add))
                    P.op("vector", lambda v, j=j: v.tensor_scalar(
                        out=fgbuf[:, 192 + j * 8: 192 + (j + 1) * 8], in0=T_all[:, j * 8:(j + 1) * 8],
                        scalar1=mask_t[:, 8 + j:9 + j], scalar2=None, op0=OP.mult))
            P.op("vector", lambda v: v.engine_nop() if False else v.tensor_copy(ctmp2[:], fgbuf[:, 0:8]), inc="s_dve")
            P.wait("vector", "s_dve")
            P.op("vector", lambda v: v.tensor_copy(c_cols[:], s_end[:]), inc="s_dve")
            P.wait("vector", "s_dve")
            for j in range(8):
                cop(lambda v, j=j: v.tensor_tensor(out=ctmp[:], in0=c_cols[:], in1=fgbuf[:, j * 8:(j + 1) * 8], op=OP.mult))
                cop(lambda v, j=j: v.tensor_tensor(out=c_cols[:], in0=ctmp[:], in1=fgbuf[:, 64 + j * 8: 64 + (j + 1) * 8], op=OP.add))
            if q == 0:
                for j in range(8):
                    cop(lambda v, j=j: v.tensor_tensor(out=ctmp2[:], in0=s_end[:], in1=fgbuf[:, 128 + j * 8: 128 + (j + 1) * 8], op=OP.mult))
                    cop(lambda v, j=j: v.tensor_tensor(out=s_end[:], in0=ctmp2[:], in1=fgbuf[:, 192 + j * 8: 192 + (j + 1) * 8], op=OP.add))
            P.op("vector", lambda v: v.tensor_copy(ctmp[:], c_cols[:]), inc="s_dve")
            x_ctr = 0
            carry_done = P.cnt["s_dve"]
            if q == 0 and "carry" in d_dbg:
                P.wait("sync", "s_dve", carry_done)
                P.dma("sync", lambda s: s.dma_start(out=d_dbg["carry"][0:128, 0:8], in_=Tloc[:]), "s_tall")
                P.dma("sync", lambda s: s.dma_start(out=d_dbg["carry"][0:128, 8:72], in_=T_all[:]), "s_tall")
                P.dma("sync", lambda s: s.dma_start(out=d_dbg["carry"][0:128, 72:80], in_=c_cols[:]), "s_tall")
                P.dma("sync", lambda s: s.dma_start(out=d_dbg["carry"][0:128, 80:88], in_=s_end[:]), "s_tall")
                P.dma("sync", lambda s: s.dma_start(out=d_dbg["carry"][0:128, 88:96], in_=D1024[:]), "s_tall")
            P.phase = "rescan"
            P.wait("vector", "s_dve", carry_done)
            def rescan(p):
                for c in range(2):
                    init = c_cols[:, p:p + 1] if c == 0 else u_sl(p, c * 512 - 1, 1)
                    P.op("vector", lambda v, p=p, c=c, init=init: v.tensor_tensor_scan(
                        out=u_sl(p, c * 512), data0=d_cols[:, p:p + 1].broadcast_to((128, 512)),
                        data1=u_sl(p, c * 512), initial=init, op0=OP.mult, op1=OP.add),
                        inc="s_dve")
                    P.wait("vector", "s_dve")

            P.phase = "p3"
            q_copied = {}
            for m in range(MH):
                sl, key, wt = load_col(dw["w_q"][m])
                P.wait("tensor", wt[0], wt[1])
                q_bs = pair_group(sl, ln_rhs)
                for i in range(NTILE):
                    bq, pe_q = q_bs[i]
                    P.wait("scalar", "s_pe", pe_q)
                    if m == 0 and i == 0:
                        P.wait("scalar", "s_dve", dve_T)
                    P.op("scalar", lambda s, m=m, i=i, bq=bq: s.activation(
                        out=q16[:, (m * NTILE + i) * 512:(m * NTILE + i + 1) * 512],
                        in_=pbank[bq][:], func=AF.Copy), inc="s_act")
                    q_copied[(m, i)] = P.cnt["s_act"]
                    bank_free[bq] = ("s_act", P.cnt["s_act"])
                slot_free[key] = P.cnt["s_pe"]

            read_done = {}
            for m in range(MH):
                n = m // 2
                rescan(m)
                if m % 2 == 0:
                    for i in range(NTILE):
                        psn = (PS_A, PS_B)[i]
                        sem, val = bank_free[psn]
                        P.wait("tensor", sem, val)
                        P.wait("tensor", "s_dve", mix_done[i])
                        P.op("tensor", lambda t, i=i, n=n, psn=psn: t.matmul(
                            pbank[psn][:], ones1f[:],
                            w4[0:1, i * 2048 + n * 512: i * 2048 + (n + 1) * 512],
                            start=True, stop=True), inc="s_pe")
                    smb_pe = P.cnt["s_pe"]
                for i in range(NTILE):
                    par = (m * NTILE + i) % 2
                    P.wait("vector", "s_act", q_copied[(m, i)])
                    P.wait("vector", "s_act", tmpa_free[par])
                    P.op("vector", lambda v, m=m, i=i, par=par: v.tensor_tensor(
                        out=tmp_a[par][:], in0=q16[:, (m * NTILE + i) * 512:(m * NTILE + i + 1) * 512],
                        in1=u_sl(m, i * TT), op=OP.mult), inc="s_dve")
                    dv = P.cnt["s_dve"]
                    P.wait("scalar", "s_dve", dv)
                    P.op("scalar", lambda s, par=par: s.activation(
                        out=tmp_a[par][:], in_=tmp_a[par][:], func=AF.Silu), inc="s_act")
                    av = P.cnt["s_act"]
                    tmpa_free[par] = av
                    P.wait("vector", "s_act", av)
                    P.wait("vector", "s_pe", smb_pe)
                    psn = (PS_A, PS_B)[i]
                    P.op("vector", lambda v, m=m, i=i, par=par, psn=psn: v.tensor_tensor(
                        out=read16[:, (m * NTILE + i) * 512:(m * NTILE + i + 1) * 512],
                        in0=tmp_a[par][:], in1=pbank[psn][:], op=OP.mult), inc="s_dve")
                    read_done[(m, i)] = P.cnt["s_dve"]
                    if m % 2 == 1:
                        bank_free[psn] = ("s_dve", P.cnt["s_dve"])

            P.wait("tensor", "s_dve", ug_done)
            ln2_done = []

            def ln2_stage_cast(i, resid1_t):
                t0 = i * TT
                for p in range(8):
                    P.wait("scalar", "s_dve", resid1_t)
                    P.op("scalar", lambda s, p=p, t0=t0: s.activation(
                        out=x16[:, p * 512:(p + 1) * 512], in_=x_sl(p, t0), func=AF.Copy))
                    P.op("scalar", lambda s, p=p, t0=t0: s.activation(
                        out=sq16[:, p * 512:(p + 1) * 512], in_=x_sl(p, t0), func=AF.Square),
                        inc="s_act")
                return P.cnt["s_act"]

            def ln2_stage_stats(act_cast):
                P.wait("tensor", "s_act", act_cast)
                sem, val = bank_free[PS_MEAN]; P.wait("tensor", sem, val)
                sem, val = bank_free[PS_MSQ]; P.wait("tensor", sem, val)
                for p in range(8):
                    P.op("tensor", lambda t, p=p: t.matmul(
                        pbank[PS_MEAN][0:1, :], onesP16[:, 0:1], x16[:, p * 512:(p + 1) * 512],
                        start=(p == 0), stop=(p == 7)), inc="s_pe" if p == 7 else None, n=1)
                for p in range(8):
                    P.op("tensor", lambda t, p=p: t.matmul(
                        pbank[PS_MSQ][0:1, :], onesP16[:, 0:1], sq16[:, p * 512:(p + 1) * 512],
                        start=(p == 0), stop=(p == 7)), inc="s_pe" if p == 7 else None, n=1)
                pe_stats = P.cnt["s_pe"]
                P.wait("vector", "s_pe", pe_stats)
                P.op("vector", lambda v: v.tensor_scalar(out=m_row[:], in0=pbank[PS_MEAN][0:1, :],
                                                         scalar1=1.0 / H, scalar2=None, op0=OP.mult))
                P.op("vector", lambda v: v.tensor_scalar(out=q_row[:], in0=pbank[PS_MSQ][0:1, :],
                                                         scalar1=1.0 / H, scalar2=None, op0=OP.mult))
                P.op("vector", lambda v: v.tensor_tensor(out=v_row[:], in0=m_row[:], in1=m_row[:], op=OP.mult))
                P.op("vector", lambda v: v.tensor_tensor(out=v_row[:], in0=q_row[:], in1=v_row[:], op=OP.subtract),
                     inc="s_dve")
                dve_var = P.cnt["s_dve"]
                bank_free[PS_MEAN] = ("s_dve", dve_var)
                bank_free[PS_MSQ] = ("s_dve", dve_var)
                P.wait("scalar", "s_dve", dve_var)
                P.op("scalar", lambda s: s.activation(out=r_row[:], in_=v_row[:], func=AF.Ln,
                                                      bias=epsc[0:1, 0:1]))
                P.op("scalar", lambda s: s.activation(out=r_row[:], in_=r_row[:], func=AF.Exp,
                                                      scale=nhalf[0:1, 0:1]), inc="s_act")
                act_rstd = P.cnt["s_act"]
                P.wait("vector", "s_act", act_rstd)
                P.op("vector", lambda v: v.tensor_tensor(out=b_row[:], in0=m_row[:], in1=r_row[:], op=OP.mult))
                P.op("vector", lambda v: v.tensor_scalar(out=b_row[:], in0=b_row[:],
                                                         scalar1=-1.0, scalar2=None, op0=OP.mult), inc="s_dve")
                return P.cnt["s_dve"]

            def ln2_stage_bcast(dve_b):
                P.wait("tensor", "s_dve", dve_b)
                sem, val = bank_free[PS_A]; P.wait("tensor", sem, val)
                sem, val = bank_free[PS_B]; P.wait("tensor", sem, val)
                P.op("tensor", lambda t: t.matmul(pbank[PS_A][:], ones1f[:], r_row[:],
                                                  start=True, stop=True), inc="s_pe")
                P.op("tensor", lambda t: t.matmul(pbank[PS_B][:], ones1f[:], b_row[:],
                                                  start=True, stop=True), inc="s_pe")
                return P.cnt["s_pe"]

            def ln2_stage_apply(i, pe_bc, ps):
                t0 = i * TT
                P.wait("vector", "s_pe", pe_bc)
                for p in ps:
                    P.op("vector", lambda v, p=p, t0=t0: v.tensor_tensor(
                        out=tmp_a[0][:], in0=x_sl(p, t0), in1=pbank[PS_A][:], op=OP.mult))
                    P.op("vector", lambda v, p=p, t0=t0: v.tensor_tensor(
                        out=ln_sl(p, t0), in0=tmp_a[0][:], in1=pbank[PS_B][:], op=OP.add),
                        inc="s_dve" if p == ps[-1] else None, n=1)
                return P.cnt["s_dve"]

            ln2_st = {}
            head_banks = {}
            for m in range(4):
                sub_r = wslot3[m % 2][:, (m // 2) * 1024:(m // 2) * 1024 + 1024]
                key3 = ("w3", m % 2)
                P.wait("gpsimd", "s_pe", slot_free[key3])
                wt_r = P.dma("gpsimd", lambda g, sub_r=sub_r, m=m: g.dma_start(
                    out=sub_r, in_=dw["w_rout"][m]), f"s_w3s{m % 2}")
                sl_d, key_d, wt_d = load_col(dw["w_down"][m])
                b = ROT[rot_idx[0] % len(ROT)]; rot_idx[0] += 1
                sem, val = bank_free[b]; P.wait("tensor", sem, val)
                P.wait("tensor", wt_d[0], wt_d[1])
                for k in range(KH):
                    P.op("tensor", lambda t, sl=sl_d, k=k, b=b,
                         first=(k == 0):
                         t.matmul(pbank[b][:], sl[:, k * 128:(k + 1) * 128],
                                  ug16[:, (k * NTILE) * 512:(k * NTILE + 1) * 512],
                                  start=first, stop=False), n=1)
                head_banks[m] = (b, sub_r, (f"s_w3s{m % 2}", wt_r), key_d)
            for k in range(KH):
                P.wait("tensor", "s_dve", read_done[(k, 0)])
                for m in range(4):
                    b, sub_r, wtr, key_d = head_banks[m]
                    P.wait("tensor", wtr[0])
                    P.op("tensor", lambda t, sub_r=sub_r, k=k, b=b, last=(k == KH - 1):
                         t.matmul(pbank[b][:], sub_r[:, k * 128:(k + 1) * 128],
                                  read16[:, (k * NTILE) * 512:(k * NTILE + 1) * 512],
                                  start=False, stop=last),
                         inc="s_pe" if k == KH - 1 else None, n=1)
            for m in range(4):
                b, sub_r, wtr, key_d = head_banks[m]
                P.wait("vector", "s_pe")
                P.op("vector", lambda v, m=m, b=b: v.tensor_tensor(
                    out=x_sl(m, 0), in0=x_sl(m, 0), in1=pbank[b][:], op=OP.add),
                    inc="s_dve")
                bank_free[b] = ("s_dve", P.cnt["s_dve"])
                slot_free[("w3", m % 2)] = P.cnt["s_pe"]
                slot_free[key_d] = P.cnt["s_pe"]
            P.wait("tensor", "s_dve", read_done[(MH - 1, NTILE - 1)])
            for i in range(NTILE):
                for m in range(MH):
                    if i == 0 and m < 4:
                        continue
                    if i == 1:
                        if m == 0:
                            ln2_st["cast"] = ln2_stage_cast(0, resid1_t0)
                        elif m == 2:
                            ln2_st["db"] = ln2_stage_stats(ln2_st["cast"])
                        elif m == 3:
                            ln2_st["bc"] = ln2_stage_bcast(ln2_st["db"])
                        elif m == 5:
                            ln2_stage_apply(0, ln2_st["bc"], [0, 1, 2, 3])
                        elif m == 7:
                            dv = ln2_stage_apply(0, ln2_st["bc"], [4, 5, 6, 7])
                            ln2_done.append(dv)
                            bank_free[PS_A] = ("s_dve", dv)
                            bank_free[PS_B] = ("s_dve", dv)
                    sl_r, key_r, wt_r = load_col(dw["w_rout"][m])
                    sl_d, key_d, wt_d = load_col(dw["w_down"][m])
                    P.wait("tensor", wt_d[0], wt_d[1])
                    b = ROT[rot_idx[0] % len(ROT)]; rot_idx[0] += 1
                    sem, val = bank_free[b]; P.wait("tensor", sem, val)
                    for k in range(KH):
                        P.op("tensor", lambda t, sl=sl_r, k=k, b=b, i=i,
                             first=(k == 0):
                             t.matmul(pbank[b][:], sl[:, k * 128:(k + 1) * 128],
                                      read16[:, (k * NTILE + i) * 512:(k * NTILE + i + 1) * 512],
                                      start=first, stop=False), n=1)
                    for k in range(KH):
                        P.op("tensor", lambda t, sl=sl_d, k=k, b=b, i=i,
                             last=(k == 7):
                             t.matmul(pbank[b][:], sl[:, k * 128:(k + 1) * 128],
                                      ug16[:, (k * NTILE + i) * 512:(k * NTILE + i + 1) * 512],
                                      start=False, stop=last),
                             inc="s_pe" if k == 7 else None, n=1)
                    pe_v = P.cnt["s_pe"]
                    P.wait("vector", "s_pe", pe_v)
                    P.op("vector", lambda v, m=m, i=i, b=b: v.tensor_tensor(
                        out=x_sl(m, i * TT), in0=x_sl(m, i * TT), in1=pbank[b][:], op=OP.add),
                        inc="s_dve")
                    bank_free[b] = ("s_dve", P.cnt["s_dve"])
                    slot_free[key_r] = P.cnt["s_pe"]
                    slot_free[key_d] = P.cnt["s_pe"]
                if i == 0:
                    resid1_t0 = P.cnt["s_dve"]

            P.phase = "ffna"
            def single_group(sl, rhs_fn, i, k_chunks=KH):
                b = ROT[rot_idx[0] % len(ROT)]; rot_idx[0] += 1
                sem, val = bank_free[b]
                P.wait("tensor", sem, val)
                for k in range(k_chunks):
                    P.op("tensor", lambda t, sl=sl, k=k, rh=rhs_fn(k, i), b=b,
                         first=(k == 0), last=(k == k_chunks - 1):
                         t.matmul(pbank[b][:], sl[:, k * 128:(k + 1) * 128], rh,
                                  start=first, stop=last),
                         inc="s_pe" if k == k_chunks - 1 else None, n=1)
                return b, P.cnt["s_pe"]

            def ffna_entry(m, tiles):
                sl_g, key_g, wt_g = load_col(dw["f_gate"][m])
                sl_u, key_u, wt_u = load_col(dw["f_up"][m])
                for i in tiles:
                    P.wait("tensor", "s_dve", ln2_done[i])
                if len(tiles) == 2:
                    P.wait("tensor", wt_g[0], wt_g[1])
                    g_bs = pair_group(sl_g, ln_rhs)
                    P.wait("tensor", wt_u[0], wt_u[1])
                    u_bs = pair_group(sl_u, ln_rhs)
                else:
                    P.wait("tensor", wt_g[0], wt_g[1])
                    g_bs = [single_group(sl_g, ln_rhs, tiles[0])]
                    P.wait("tensor", wt_u[0], wt_u[1])
                    u_bs = [single_group(sl_u, ln_rhs, tiles[0])]
                for idx, i in enumerate(tiles):
                    bG, pe_g = g_bs[idx]
                    bU, pe_u = u_bs[idx]
                    par = (m * NTILE + i) % 2
                    P.wait("scalar", "s_pe", pe_g)
                    P.wait("scalar", "s_dve", tmpb_free[par])
                    P.op("scalar", lambda s, bG=bG, par=par: s.activation(
                        out=tmp_b[par][:], in_=pbank[bG][:], func=AF.Silu,
                        scale=sc1[:, 0:1]), inc="s_act")
                    av = P.cnt["s_act"]
                    bank_free[bG] = ("s_act", av)
                    P.wait("vector", "s_act", av)
                    P.wait("vector", "s_pe", pe_u)
                    P.op("vector", lambda v, m=m, i=i, bU=bU, par=par: v.tensor_tensor(
                        out=h16[:, (m * NTILE + i) * 512:(m * NTILE + i + 1) * 512],
                        in0=tmp_b[par][:], in1=pbank[bU][:], op=OP.mult), inc="s_dve")
                    dv = P.cnt["s_dve"]
                    bank_free[bU] = ("s_dve", dv)
                    tmpb_free[par] = dv
                slot_free[key_g] = P.cnt["s_pe"]
                slot_free[key_u] = P.cnt["s_pe"]

            resid1_t1 = P.cnt["s_dve"]
            ln2_st["cast1"] = ln2_stage_cast(1, resid1_t1)
            ffna_entry(0, (0,))
            ln2_st["db1"] = ln2_stage_stats(ln2_st["cast1"])
            ffna_entry(1, (0,))
            ln2_st["bc1"] = ln2_stage_bcast(ln2_st["db1"])
            ffna_entry(2, (0,))
            ln2_stage_apply(1, ln2_st["bc1"], [0, 1, 2, 3])
            ffna_entry(3, (0,))
            dv = ln2_stage_apply(1, ln2_st["bc1"], [4, 5, 6, 7])
            ln2_done.append(dv)
            bank_free[PS_A] = ("s_dve", dv)
            bank_free[PS_B] = ("s_dve", dv)
            for m in range(4, MHID):
                ffna_entry(m, (0, 1))
            for m in range(4):
                ffna_entry(m, (1,))

            P.phase = "ffnb"
            P.wait("tensor", "s_dve")
            def h_rhs(k, i):
                return h16[:, (k * NTILE + i) * 512:(k * NTILE + i + 1) * 512]
            for m in range(MH):
                sl, key, wt = load_col(dw["f_down"][m], is3072=True)
                P.wait("tensor", wt[0], wt[1])
                for i, (b, pe_v) in enumerate(pair_group(sl, h_rhs, k_chunks=MHID)):
                    P.wait("vector", "s_pe", pe_v)
                    P.op("vector", lambda v, m=m, i=i, b=b: v.scalar_tensor_tensor(
                        out=x_sl(m, i * TT), in0=pbank[b][:], scalar=sc2[:, 0:1],
                        in1=x_sl(m, i * TT), op0=OP.mult, op1=OP.add), inc="s_dve")
                    bank_free[b] = ("s_dve", P.cnt["s_dve"])
                P.wait("sync", "s_dve", P.cnt["s_dve"])
                store_tgt[m] = P.dma("sync", lambda s, m=m, q=q: s.dma_start(
                    out=d_y[q][:, m * TPASS:(m + 1) * TPASS],
                    in_=x_rs[:, m * TPASS:(m + 1) * TPASS]), f"s_y{m}")
                slot_free[key] = P.cnt["s_pe"]
            resid2_done = P.cnt["s_dve"]

            if q == 0 and d_dbg:
                P.wait("sync", "s_dve")
                for nm, dst in d_dbg.items():
                    src = {"ln16": None, "ubuf": u_buf, "states": u_buf,
                           "read16": None, "xr1": x_rs, "ug16": None}.get(nm)
                    if src is not None:
                        P.dma("sync", lambda s, src=src, dst=dst: s.dma_start(out=dst, in_=src[:]), "s_tall")


        for p in range(8):
            P.wait("sync", f"s_y{p}")

        @blk.sync
        def _(engine):
            for f in P.q["sync"]:
                f(engine)

        @blk.gpsimd
        def _(engine):
            for f in P.q["gpsimd"]:
                f(engine)

        @blk.vector
        def _(engine):
            for f in P.q["vector"]:
                f(engine)

        @blk.scalar
        def _(engine):
            for f in P.q["scalar"]:
                f(engine)

        @blk.tensor
        def _(engine):
            for f in P.q["tensor"]:
                f(engine)

    nc._plan = P
    return nc



def _img_sq(w):
    t = np.asarray(w).T.reshape(8, 128, 8, 128)
    return np.ascontiguousarray(t.transpose(2, 1, 0, 3).reshape(8, 128, 1024))


def _img_up(w):
    t = np.asarray(w).T.reshape(8, 128, 24, 128)
    return np.ascontiguousarray(t.transpose(2, 1, 0, 3).reshape(24, 128, 1024))


def _img_down(w):
    t = np.asarray(w).T.reshape(24, 128, 8, 128)
    return np.ascontiguousarray(t.transpose(2, 1, 0, 3).reshape(8, 128, 3072))


def make_inputs(core, ins):
    b, g = core // 4, core % 4
    x = np.asarray(ins["L0_attn_x"])
    xb = x[b]
    xi = np.empty((NPASS, 8, 128, TPASS), np.float32)
    for qq in range(NPASS):
        toks = xb[qq * 4096 + g * 1024: qq * 4096 + (g + 1) * 1024, :]
        xi[qq] = toks.T.reshape(8, 128, TPASS)
    ldf = np.asarray(ins["L0_attn_logit_decay"], np.float32).reshape(1024)
    inf_ = np.asarray(ins["L0_attn_initial_state"], np.float32).reshape(1024)
    wmix = np.asarray(ins["L0_attn_W_mix"], np.float32)
    wmix_img = np.ascontiguousarray(wmix.T.reshape(8, 128, 4).transpose(1, 0, 2).reshape(128, 32))
    mA = np.array([1.0 if (j // 4 == b and j % 4 < g) else 0.0 for j in range(8)], np.float32)
    mE = np.array([1.0 if j // 4 == b else 0.0 for j in range(8)], np.float32)
    mask = np.tile(np.concatenate([mA, mE])[None, :], (128, 1)).astype(np.float32)
    return {
        "x_img": xi,
        "w_in": _img_sq(ins["L0_attn_W_in"]).astype(np.float32),
        "w_val": _img_sq(ins["L0_attn_W_val"]).astype(np.float32),
        "w_q": _img_sq(ins["L0_attn_W_q"]).astype(np.float32),
        "w_rout": _img_sq(ins["L0_attn_W_rout"]).astype(np.float32),
        "w_up": _img_sq(ins["L0_attn_W_up"]).astype(np.float32),
        "w_gate": _img_sq(ins["L0_attn_W_gate"]).astype(np.float32),
        "w_down": _img_sq(ins["L0_attn_W_down"]).astype(np.float32),
        "f_gate": _img_up(ins["L0_ffn_w_gate"]).astype(np.int32),
        "f_up": _img_up(ins["L0_ffn_w_up"]).astype(np.int32),
        "f_down": _img_down(ins["L0_ffn_w_down"]).astype(np.int32),
        "wmix_img": wmix_img,
        "ld_cols": np.ascontiguousarray(ldf.reshape(8, 128).T),
        "init_cols": np.ascontiguousarray(inf_.reshape(8, 128).T),
        "mask_img": mask,
    }


def assemble(results):
    out = np.empty((B, S, H), np.float32)
    for core in range(8):
        b, g = core // 4, core % 4
        y = results[core]["y_img"]
        for qq in range(NPASS):
            blkv = y[qq].reshape(128, 8, TPASS).transpose(1, 0, 2).reshape(1024, TPASS)
            out[b, qq * 4096 + g * 1024: qq * 4096 + (g + 1) * 1024, :] = blkv.T
    return out


_NC_CACHE = {}


def kernel(**inputs):
    key = "main"
    if key not in _NC_CACHE:
        _NC_CACHE[key] = build_nc()
    nc = _NC_CACHE[key]
    in_maps = [make_inputs(c, inputs) for c in range(8)]
    res = run_bass_kernel_spmd(nc, in_maps, core_ids=list(range(8)))
    return assemble(res.results)
